# version 06 (final)
# baseline (speedup 1.0000x reference)
import numpy as np
import ml_dtypes
from contextlib import ExitStack
import concourse.bass as bass
import concourse.mybir as mybir
from concourse.bass_utils import run_bass_kernel_spmd

F32 = mybir.dt.float32
BF16 = mybir.dt.bfloat16
AX = mybir.AxisListType.X
ALU = mybir.AluOpType
AF = mybir.ActivationFunctionType

B = 4
S = 2048
D = 1024
NT = S // 128

QUEUES = ("tensor", "vector", "scalar", "gpsimd", "sync")

class _Op:
    __slots__ = ("queue", "fn", "deps", "dma", "needed", "value", "tag")
    def __init__(self, queue, fn, dma, tag):
        self.queue, self.fn, self.dma, self.tag = queue, fn, dma, tag
        self.deps, self.needed, self.value = [], False, None

PARANOID = False

class Plan:
    def __init__(self):
        self.ops = {q: [] for q in QUEUES}
        self.buf = {}
        self.pending = {q: [] for q in QUEUES}
        self.last = {}

    def op(self, queue, fn, reads=(), writes=(), dma=False, tag="", after=()):
        if PARANOID:
            self.barrier()
        o = _Op(queue, fn, dma, tag)
        deps = list(after)
        for k in reads:
            e = self.buf.get(k)
            if e and e["w"] is not None:
                deps.append(e["w"])
        for k in writes:
            e = self.buf.get(k)
            if e:
                if e["w"] is not None:
                    deps.append(e["w"])
                deps.extend(e["r"])
        if self.pending[queue]:
            deps.extend(self.pending[queue])
            self.pending[queue] = []
        o.deps = [d for d in dict.fromkeys(deps) if d is not o]
        for k in reads:
            self.buf.setdefault(k, {"w": None, "r": []})["r"].append(o)
        for k in writes:
            self.buf[k] = {"w": o, "r": []}
        self.ops[queue].append(o)
        self.last[(queue, dma)] = o
        return o

    def settle(self, c, keys, tag="settle"):
        return self.op("vector", lambda e: e.tensor_copy(c.ST[:, 254:255], c.ST[:, 253:254]),
                       writes=list(keys), tag=tag)

    def barrier(self):
        lasts = [op for op in self.last.values() if op is not None]
        for q in QUEUES:
            self.pending[q] = list(lasts)
        self.buf = {}

    def emit(self, nc, block, sems, echo_fns=None):
        if echo_fns:
            for q in ("vector", "scalar"):
                needs_echo = set()
                for o in self.ops["tensor"]:
                    for d in o.deps:
                        if d.queue == q and not d.dma:
                            needs_echo.add(id(d))
                if not needs_echo:
                    continue
                new_list = []
                echo_of = {}
                for o in self.ops[q]:
                    new_list.append(o)
                    if id(o) in needs_echo:
                        e_op = _Op(q, echo_fns[q], False, o.tag + "#echo")
                        new_list.append(e_op)
                        echo_of[id(o)] = e_op
                self.ops[q] = new_list
                for o in self.ops["tensor"]:
                    o.deps = [echo_of.get(id(d), d) if (d.queue == q and not d.dma) else d
                              for d in o.deps]
        for q in QUEUES:
            for o in self.ops[q]:
                for d in o.deps:
                    d.needed = True
        for ol in self.pending.values():
            for d in ol:
                d.needed = True
        for q in QUEUES:
            cnt = {False: 0, True: 0}
            for o in self.ops[q]:
                if o.dma:
                    o.needed = True
                if o.needed:
                    cnt[o.dma] += 16 if o.dma else 1
                    o.value = cnt[o.dma]

        def emit_queue(eng, q):
            waited = {}
            for o in self.ops[q]:
                need = {}
                for d in o.deps:
                    kk = (d.queue, d.dma)
                    if d.value is None:
                        raise RuntimeError(f"dep of {o.tag} ({d.tag}) has no token")
                    if need.get(kk, 0) < d.value:
                        need[kk] = d.value
                for kk, val in sorted(need.items()):
                    if waited.get(kk, 0) < val:
                        eng.wait_ge(sems[kk], val)
                        waited[kk] = val
                inst = o.fn(eng)
                if o.needed:
                    if inst is None:
                        raise RuntimeError(f"op {o.tag} has dependents but returned no instruction")
                    inst.then_inc(sems[(o.queue, o.dma)], 16 if o.dma else 1)

        @block.tensor
        def _(e): emit_queue(e, "tensor")
        @block.vector
        def _(e): emit_queue(e, "vector")
        @block.scalar
        def _(e): emit_queue(e, "scalar")
        @block.gpsimd
        def _(e): emit_queue(e, "gpsimd")
        @block.sync
        def _(e): emit_queue(e, "sync")

def make_sems(nc, E):
    sems = {}
    for q in QUEUES:
        sems[(q, False)] = E(nc.semaphore(f"sem_{q}_c"))
        sems[(q, True)] = E(nc.semaphore(f"sem_{q}_d"))
    return sems

class Ctx:
    pass

def alloc_common(nc, E):
    c = Ctx()
    c.nc = nc
    c.ps = [E(nc.psum_tensor(f"ps{i}", [128, 512], F32)) for i in range(6)]
    c.pt = [E(nc.psum_tensor(f"pt{i}", [128, 1024], BF16)) for i in range(2)]
    c.XLNF = E(nc.sbuf_tensor("XLNF", [128, 16384], BF16))
    c.B1 = E(nc.sbuf_tensor("B1", [128, 16384], BF16))
    c.B2 = E(nc.sbuf_tensor("B2", [128, 16384], BF16))
    c.B3 = E(nc.sbuf_tensor("B3", [128, 16704], BF16))
    c.WB = E(nc.sbuf_tensor("WB", [128, 10240], BF16))
    c.FX = E(nc.sbuf_tensor("FX", [128, 4096], F32))
    c.FW = E(nc.sbuf_tensor("FW", [128, 4096], F32))
    c.ST = E(nc.sbuf_tensor("ST", [128, 512], F32))
    c.CONST = E(nc.sbuf_tensor("CONST", [128, 384], BF16))
    c.IDF = E(nc.sbuf_tensor("IDF", [128, 128], F32))
    c.SEL = E(nc.sbuf_tensor("SEL", [16, 1024], BF16))
    c.ident = c.CONST[:, 0:128]
    c.ones = c.CONST[:, 128:256]
    c.i2 = c.CONST[:, 256:320]
    return c

def tail_fence(e, c, n=8):
    last = None
    for _ in range(n):
        last = e.matmul(c.ps[4][0:16, 384:512], c.ident[:, 0:16], c.ident[:, 0:128],
                        start=True, stop=True)
    return last

def pe_fence(e, c):
    return e.matmul(c.ps[4][0:16, 496:512], c.ident[:, 0:16], c.ident[:, 0:16],
                    start=True, stop=True)

def make_echo_fns(c):
    return {
        "vector": lambda e: e.tensor_copy(c.ST[:, 255:256], c.ST[:, 254:255]),
        "scalar": lambda e: e.copy(c.ST[:, 253:254], c.ST[:, 252:253]),
    }

def load_consts(P, c, consts):
    P.op("sync", lambda e: e.dma_start(c.ident, consts["ident"]), writes=["ident"], dma=True, tag="ldident")
    P.op("sync", lambda e: e.dma_start(c.i2, consts["i2"]), writes=["i2"], dma=True, tag="ldi2")
    P.op("sync", lambda e: e.dma_start(c.SEL[:], consts["sel"]), writes=["sel"], dma=True, tag="ldsel")
    P.op("sync", lambda e: e.dma_start(c.IDF[:], consts["identf"]), writes=["identf"], dma=True, tag="ldidf")
    P.op("vector", lambda e: e.memset(c.ones, 1.0), writes=["ones"], tag="ones")

def emit_ln_b(P, c, x_dram_b, tag, var_stash=None, dst_tm=None, dst_tm_key=None):
    stats = c.ST[:, 0:12]
    mv = c.ST[:, 12:14]
    rstd = c.ST[:, 14:15]
    nmean = c.ST[:, 15:16]
    pend = None
    def emit_ev(pend, after_op):
        i, ptb = pend
        def ev(e, ptb=ptb, i=i):
            src = ptb[:].rearrange("p (j t) -> p j t", j=8)
            dstap = c.XLNF[:].rearrange("p (j s) -> p j s", j=8)[:, :, 128*i:128*(i+1)]
            return e.tensor_copy(dstap, src)
        P.op("vector", ev, reads=[f"pt{i%2}"], writes=["xlnfm"],
             after=[after_op] if after_op is not None else (), tag=f"{tag}ev{i}")
    for i in range(NT):
        xt = c.FX[:, 1024*(i % 2):1024*(i % 2 + 1)]
        xkey = f"{tag}x{i%2}"
        P.op("sync", lambda e, xt=xt, i=i: e.dma_start(xt, x_dram_b[128*i:128*(i+1), :]),
             writes=[xkey], dma=True, tag=f"{tag}ldx{i}")
        P.op("vector", lambda e, xt=xt: e.bn_stats(stats[:, 0:6], xt[:, 0:512]),
             reads=[xkey], writes=[f"{tag}stats"], tag=f"{tag}bnst{i}")
        P.op("vector", lambda e, xt=xt: e.bn_stats(stats[:, 6:12], xt[:, 512:1024]),
             reads=[xkey], writes=[f"{tag}stats"], tag=f"{tag}bnstb{i}")
        P.op("vector", lambda e: e.bn_aggr(mv, stats),
             reads=[f"{tag}stats"], writes=[f"{tag}mv"], tag=f"{tag}bnag{i}")
        if var_stash is not None:
            P.op("vector", lambda e, i=i: e.tensor_copy(var_stash[:, i:i+1], mv[:, 1:2]),
                 reads=[f"{tag}mv"], writes=["varstash"], tag=f"{tag}vst{i}")
        P.op("vector", lambda e: e.tensor_scalar(rstd, mv[:, 1:2], 1e-5, None, ALU.add),
             reads=[f"{tag}mv"], writes=[f"{tag}rstd"], tag=f"{tag}veps{i}")
        P.op("scalar", lambda e: e.activation(rstd, rstd, AF.Ln),
             reads=[f"{tag}rstd"], writes=[f"{tag}rstd"], tag=f"{tag}ln{i}")
        P.op("scalar", lambda e: e.activation(rstd, rstd, AF.Exp, scale=-0.5),
             reads=[f"{tag}rstd"], writes=[f"{tag}rstd"], tag=f"{tag}exp{i}")
        P.op("vector", lambda e: e.tensor_tensor(nmean, mv[:, 0:1], rstd, ALU.mult),
             reads=[f"{tag}mv", f"{tag}rstd"], writes=[f"{tag}nmean"], tag=f"{tag}nm{i}")
        P.op("vector", lambda e: e.tensor_scalar(nmean, nmean, -1.0, None, ALU.mult),
             reads=[f"{tag}nmean"], writes=[f"{tag}nmean"], tag=f"{tag}nm2{i}")
        if dst_tm is not None:
            tmslot = dst_tm[:, 1024*i:1024*(i+1)]
            tmkey = dst_tm_key
        else:
            tmslot = c.FX[:, 2048:3072].bitcast(BF16)[:, 0:1024]
            tmkey = f"{tag}tmstage"
        P.op("vector", lambda e, tmslot=tmslot, xt=xt: e.tensor_scalar(
                tmslot, xt, rstd, nmean, ALU.mult, ALU.add),
             reads=[xkey, f"{tag}nmean", f"{tag}rstd"], writes=[tmkey], tag=f"{tag}norm{i}")
        ptb = c.pt[i % 2]
        def t8(e, tmslot=tmslot, ptb=ptb):
            for j in range(8):
                e.transpose(ptb[:, 128*j:128*(j+1)], tmslot[:, 128*j:128*(j+1)], c.ident)
            return pe_fence(e, c)
        t8op = P.op("tensor", t8, reads=[tmkey, "ident"], writes=[f"pt{i%2}", "psfence"],
                    tag=f"{tag}T{i}")
        if pend is not None:
            emit_ev(pend, t8op)
        pend = (i, ptb)
    tail = P.op("tensor", lambda e: tail_fence(e, c), reads=["ident", "consttens"],
                writes=["psfence"], tag=f"{tag}tail")
    emit_ev(pend, tail)

def emit_load_w1024(P, c, w_dram, tag, wkey="wcur"):
    for r in range(8):
        stage = c.FW[:, 0:1024] if r % 2 == 0 else c.FW[:, 1024:2048]
        skey = f"wstage{r%2}"
        P.op("sync", lambda e, r=r, stage=stage: e.dma_start(stage, w_dram[128*r:128*(r+1), :]),
             writes=[skey], dma=True, tag=f"{tag}ld{r}")
        P.op("vector", lambda e, r=r, stage=stage: e.tensor_copy(c.WB[:, 1024*r:1024*(r+1)], stage),
             reads=[skey], writes=[wkey], tag=f"{tag}c{r}")

def emit_resid_store(P, c, x_dram_b, out_dram_b, i, ps_pair, ps_keys, after=(),
                     extra=None, extra_key=None, tag=""):
    xt = c.FX[:, 2048 + 1024*(i % 2):2048 + 1024*(i % 2 + 1)]
    key = f"{tag}rx{i%2}"
    ot = xt
    P.op("sync", lambda e, xt=xt, i=i: e.dma_start(xt, x_dram_b[128*i:128*(i+1), :]),
         writes=[key], dma=True, tag=f"{tag}ldr{i}")
    for h in range(2):
        sl = slice(512*h, 512*(h+1))
        P.op("vector", lambda e, h=h, sl=sl: e.tensor_add(ot[:, sl], ps_pair[h], xt[:, sl]),
             reads=[ps_keys[h], key], writes=[key], after=after, tag=f"{tag}add{i}_{h}")
        if extra is not None:
            P.op("vector", lambda e, h=h, sl=sl: e.tensor_add(ot[:, sl], ot[:, sl], extra[:, sl]),
                 reads=[key, extra_key], writes=[key], tag=f"{tag}addb{i}_{h}")
    P.op("sync", lambda e, ot=ot, i=i: e.dma_start(out_dram_b[128*i:128*(i+1), :], ot),
         reads=[key], dma=True, tag=f"{tag}st{i}")

DEBUG_STOP = None
DEBUG_STASH = None
SB_DEBUG = None
BAR_AFTER_W = False
BAR_MM = False

def emit_L1_attn_b(P, c, x_dram, out_dram, W):
    bo_bc = c.FW[:, 2048:3072]
    P.op("sync", lambda e: e.dma_start(c.FW[0:1, 3072:4096], W["bo"][None, :]),
         writes=["bostage"], dma=True, tag="ldbo")
    bo_bf = c.WB[0:1, 9216:10240]
    P.op("vector", lambda e: e.tensor_copy(bo_bf, c.FW[0:1, 3072:4096]),
         reads=["bostage"], writes=["bobf"], tag="bocast")
    def bo_mm(e):
        e.matmul(c.ps[5][:], c.ones[0:1, :], bo_bf[:, 0:512], start=True, stop=True)
        e.matmul(c.ps[3][:], c.ones[0:1, :], bo_bf[:, 512:1024], start=True, stop=True)
        return pe_fence(e, c)
    P.op("tensor", bo_mm, reads=["bobf", "ones", "ident"], writes=["ps5", "ps3", "psfence"], tag="bomm")
    bo_tail = P.op("tensor", lambda e: tail_fence(e, c), reads=["ident", "consttens"],
                   writes=["psfence"], tag="botail")
    P.op("vector", lambda e: e.tensor_copy(bo_bc[:, 0:512], c.ps[5][:]),
         reads=["ps5"], writes=["bobc"], after=[bo_tail], tag="boev0")
    P.op("vector", lambda e: e.tensor_copy(bo_bc[:, 512:1024], c.ps[3][:]),
         reads=["ps3"], writes=["bobc"], after=[bo_tail], tag="boev1")

    for b in range(B):
        xb = x_dram[S*b:S*(b+1), :]
        ob = out_dram[S*b:S*(b+1), :]
        P.barrier()
        emit_ln_b(P, c, xb, tag=f"A{b}")
        for name, wd, dst, dkey in (("q", W["Wq"], c.B1, "qtm"), ("k", W["Wk"], c.B2, "ktm")):
            emit_load_w1024(P, c, wd, tag=f"{name}w{b}")
            def emit_ev2(i, banks, after_op, dst=dst, dkey=dkey, name=name):
                for h in range(2):
                    P.op("vector", lambda e, i=i, h=h, bk=banks[h], dst=dst: e.tensor_copy(
                            dst[:, 1024*i + 512*h:1024*i + 512*(h+1)], c.ps[bk][:]),
                         reads=[f"ps{banks[h]}"], writes=[dkey], after=[after_op],
                         tag=f"{name}ev{b}_{i}_{h}")
            pend = None
            for i in range(NT):
                banks = ((2*i) % 4, (2*i + 1) % 4)
                def mm(e, i=i, banks=banks):
                    for h in range(2):
                        for r in range(8):
                            e.matmul(c.ps[banks[h]][:],
                                     c.XLNF[:, 2048*r + 128*i:2048*r + 128*(i+1)],
                                     c.WB[:, 1024*r + 512*h:1024*r + 512*(h+1)],
                                     start=(r == 0), stop=(r == 7))
                    return pe_fence(e, c)
                mmop = P.op("tensor", mm, reads=["xlnfm", "wcur", "ident"],
                            writes=[f"ps{banks[0]}", f"ps{banks[1]}", "psfence"],
                            tag=f"{name}mm{b}_{i}")
                if pend is not None:
                    emit_ev2(pend[0], pend[1], mmop)
                pend = (i, banks)
            tail = P.op("tensor", lambda e: tail_fence(e, c), reads=["ident", "consttens"],
                        writes=["psfence"], tag=f"{name}tail{b}")
            emit_ev2(pend[0], pend[1], tail)
        emit_load_w1024(P, c, W["Wv"], tag=f"vw{b}")
        pend = None
        for k in range(32):
            r, n4 = divmod(k, 4)
            bank = k % 4
            def mmv(e, r=r, n4=n4, bank=bank):
                for kk in range(8):
                    e.matmul(c.ps[bank][:],
                             c.WB[:, 1024*kk + 128*r:1024*kk + 128*(r+1)],
                             c.XLNF[:, 2048*kk + 512*n4:2048*kk + 512*(n4+1)],
                             start=(kk == 0), stop=(kk == 7))
                return pe_fence(e, c)
            mmop = P.op("tensor", mmv, reads=["xlnfm", "wcur", "ident"],
                        writes=[f"ps{bank}", "psfence"], tag=f"vmm{b}_{r}_{n4}")
            if pend is not None:
                pr, pn4, pb = pend
                P.op("vector", lambda e, pr=pr, pn4=pn4, pb=pb: e.tensor_copy(
                        c.B3[:, 2048*pr + 512*pn4:2048*pr + 512*(pn4+1)], c.ps[pb][:]),
                     reads=[f"ps{pb}"], writes=["vfm"], after=[mmop], tag=f"vev{b}_{pr}_{pn4}")
            pend = (r, n4, bank)
        tail = P.op("tensor", lambda e: tail_fence(e, c), reads=["ident", "consttens"],
                    writes=["psfence"], tag=f"vtail{b}")
        pr, pn4, pb = pend
        P.op("vector", lambda e, pr=pr, pn4=pn4, pb=pb: e.tensor_copy(
                c.B3[:, 2048*pr + 512*pn4:2048*pr + 512*(pn4+1)], c.ps[pb][:]),
             reads=[f"ps{pb}"], writes=["vfm"], after=[tail], tag=f"vev{b}_{pr}_{pn4}")
        def emm(e):
            for i in range(NT):
                q3 = c.B1[:, 1024*i:1024*(i+1)].rearrange("p (a d) -> p d a", a=16)
                k3 = c.B2[:, 1024*i:1024*(i+1)].rearrange("p (a d) -> p d a", a=16)
                for d in range(64):
                    e.matmul(c.ps[5][0:16, 0:16], q3[:, d, :], k3[:, d, :],
                             start=(i == 0 and d == 0), stop=(i == NT - 1 and d == 63))
            return pe_fence(e, c)
        emmop = P.op("tensor", emm, reads=["qtm", "ktm", "ident"], writes=["ps5", "psfence"],
                     tag=f"emm{b}")
        etail = P.op("tensor", lambda e: tail_fence(e, c), reads=["ident", "consttens"],
                     writes=["psfence"], tag=f"etail{b}")
        en = c.ST[:, 16:32]
        enx = c.ST[:, 32:48]
        mx = c.ST[:, 48:49]
        sm = c.ST[:, 49:50]
        P.op("vector", lambda e: e.tensor_copy(en[0:16, :], c.ps[5][0:16, 0:16]),
             reads=["ps5"], writes=["en"], after=[etail], tag=f"enev{b}")
        if DEBUG_STASH is not None:
            P.op("vector", lambda e, b=b: e.tensor_copy(c.B3[0:16, 16384+16*b:16384+16*(b+1)],
                                                        en[0:16, :]),
                 reads=["en"], writes=["endbg"], tag=f"endbg{b}")
        P.op("vector", lambda e: e.tensor_reduce(mx[0:16, :], en[0:16, :], AX, ALU.max, negate=True),
             reads=["en"], writes=["mx"], tag=f"enmx{b}")
        P.op("vector", lambda e: e.tensor_scalar(mx[0:16, :], mx[0:16, :], 0.125, None, ALU.mult),
             reads=["mx"], writes=["mx"], tag=f"enmx2{b}")
        P.op("scalar", lambda e: e.activation(enx[0:16, :], en[0:16, :], AF.Exp,
                                              bias=mx[0:16, :], scale=0.125),
             reads=["en", "mx"], writes=["enx"], tag=f"enexp{b}")
        P.op("vector", lambda e: e.reduce_sum(sm[0:16, :], enx[0:16, :], AX),
             reads=["enx"], writes=["sm"], tag=f"ensum{b}")
        P.op("vector", lambda e: e.reciprocal(sm[0:16, :], sm[0:16, :]),
             reads=["sm"], writes=["sm"], tag=f"enrec{b}")
        attn_bf = c.WB[0:16, 9216:9232]
        P.op("vector", lambda e: e.tensor_scalar(attn_bf, enx[0:16, :], sm[0:16, :], None, ALU.mult),
             reads=["enx", "sm"], writes=["attnbf"], tag=f"attn{b}")
        def attnT(e):
            e.transpose(c.pt[0][0:16, 0:16], attn_bf, c.ident[0:16, 0:16])
            return tail_fence(e, c)
        atop = P.op("tensor", attnT, reads=["attnbf", "ident", "consttens"],
                    writes=["pt0", "psfence"], tag=f"attnT{b}")
        attn_t = c.WB[0:16, 9232:9248]
        P.op("vector", lambda e: e.tensor_copy(attn_t, c.pt[0][0:16, 0:16]),
             reads=["pt0"], writes=["attnt"], tag=f"attnTev{b}")
        abc = c.ST[:, 64:192]
        pend = None
        for cc in range(8):
            bank = cc % 4
            def abmm(e, cc=cc, bank=bank):
                e.matmul(c.ps[bank][:, 0:16], c.SEL[:, 128*cc:128*(cc+1)], attn_t,
                         start=True, stop=True)
                return pe_fence(e, c)
            abop = P.op("tensor", abmm, reads=["sel", "attnt", "ident"],
                        writes=[f"ps{bank}", "psfence"], tag=f"ab{b}_{cc}")
            if pend is not None:
                pc, pb = pend
                P.op("vector", lambda e, pc=pc, pb=pb: e.tensor_copy(
                        abc[:, 16*pc:16*(pc+1)], c.ps[pb][:, 0:16]),
                     reads=[f"ps{pb}"], writes=["abc"], after=[abop], tag=f"abev{b}_{pc}")
            pend = (cc, bank)
        abtail = P.op("tensor", lambda e: tail_fence(e, c), reads=["ident", "consttens"],
                      writes=["psfence"], tag=f"abtail{b}")
        pc, pb = pend
        P.op("vector", lambda e, pc=pc, pb=pb: e.tensor_copy(
                abc[:, 16*pc:16*(pc+1)], c.ps[pb][:, 0:16]),
             reads=[f"ps{pb}"], writes=["abc"], after=[abtail], tag=f"abev{b}_{pc}")
        P.barrier()
        def lbuf(cp):
            return c.XLNF[:, 1024*cp:1024*(cp + 1)]
        def lkeys(cp):
            return [f"lmat{cp}_{cc}_{hh}" for cc in range(8) for hh in range(2)]
        def build_L(cp):
            lm = lbuf(cp)
            for cc in range(8):
                for hh in range(2):
                    a_head = 2*cp + hh
                    P.op("scalar", lambda e, cc=cc, hh=hh, a_head=a_head, lm=lm: e.activation(
                            lm[:, 128*cc + 64*hh:128*cc + 64*hh + 64],
                            c.i2, AF.Copy, scale=abc[:, 16*cc + a_head:16*cc + a_head + 1]),
                         reads=["i2", "abc"], writes=[f"lmat{cp}_{cc}_{hh}"], tag=f"L{b}_{cp}_{cc}_{hh}")
        build_L(0)
        build_L(1)
        pend = None
        for k in range(32):
            cp, n4 = divmod(k, 4)
            bank = k % 4
            if n4 == 0 and cp >= 1 and cp < 7:
                build_L(cp + 1)
            def mmx(e, cp=cp, n4=n4, bank=bank):
                lm = lbuf(cp)
                for cc in range(8):
                    e.matmul(c.ps[bank][:], lm[:, 128*cc:128*(cc+1)],
                             c.B3[:, 2048*cc + 512*n4:2048*cc + 512*(n4+1)],
                             start=(cc == 0), stop=(cc == 7))
                return pe_fence(e, c)
            mmop = P.op("tensor", mmx, reads=lkeys(cp) + ["vfm", "ident"],
                        writes=[f"ps{bank}", "psfence"], tag=f"mix{b}_{cp}_{n4}")
            if pend is not None:
                pcp, pn4, pb = pend
                P.op("vector", lambda e, pcp=pcp, pn4=pn4, pb=pb: e.tensor_copy(
                        c.B1[:, 2048*pcp + 512*pn4:2048*pcp + 512*(pn4+1)], c.ps[pb][:]),
                     reads=[f"ps{pb}"], writes=["outfm"], after=[mmop], tag=f"mixev{b}_{pcp}_{pn4}")
            pend = (cp, n4, bank)
        mtail = P.op("tensor", lambda e: tail_fence(e, c), reads=["ident", "consttens"],
                     writes=["psfence"], tag=f"mixtail{b}")
        pcp, pn4, pb = pend
        P.op("vector", lambda e, pcp=pcp, pn4=pn4, pb=pb: e.tensor_copy(
                c.B1[:, 2048*pcp + 512*pn4:2048*pcp + 512*(pn4+1)], c.ps[pb][:]),
             reads=[f"ps{pb}"], writes=["outfm"], after=[mtail], tag=f"mixev{b}_{pcp}_{pn4}")
        P.barrier()
        for a in range(16):
            cp = a // 2
            src3 = c.B1[:, 2048*cp:2048*(cp+1)].rearrange("p (q m) -> p m q", m=16)
            for m in range(16):
                hc = m // 2
                P.op("vector", lambda e, a=a, m=m, cp=cp, hc=hc, src3=src3: e.tensor_copy(
                        c.B2[64*(m % 2):64*(m % 2) + 64, 2048*hc + 128*a:2048*hc + 128*(a+1)],
                        src3[64*(a % 2):64*(a % 2) + 64, m, :]),
                     reads=["outfm"], writes=["outflat"], tag=f"scr{b}_{a}_{m}")
        P.barrier()
        emit_load_w1024(P, c, W["Wo"], tag=f"ow{b}")
        pend = None
        for i in range(NT):
            banks = ((2*i) % 4, (2*i + 1) % 4)
            def mmo(e, i=i, banks=banks):
                for h in range(2):
                    for r in range(8):
                        e.matmul(c.ps[banks[h]][:],
                                 c.B2[:, 2048*r + 128*i:2048*r + 128*(i+1)],
                                 c.WB[:, 1024*r + 512*h:1024*r + 512*(h+1)],
                                 start=(r == 0), stop=(r == 7))
                return pe_fence(e, c)
            mmop = P.op("tensor", mmo, reads=["outflat", "wcur", "ident"],
                        writes=[f"ps{banks[0]}", f"ps{banks[1]}", "psfence"], tag=f"omm{b}_{i}")
            if pend is not None:
                pi, pbk = pend
                emit_resid_store(P, c, xb, ob, pi, (c.ps[pbk[0]][:], c.ps[pbk[1]][:]),
                                 (f"ps{pbk[0]}", f"ps{pbk[1]}"), after=[mmop],
                                 extra=bo_bc, extra_key="bobc", tag=f"r{b}_")
            pend = (i, banks)
        otail = P.op("tensor", lambda e: tail_fence(e, c), reads=["ident", "consttens"],
                     writes=["psfence"], tag=f"otail{b}")
        pi, pbk = pend
        emit_resid_store(P, c, xb, ob, pi, (c.ps[pbk[0]][:], c.ps[pbk[1]][:]),
                         (f"ps{pbk[0]}", f"ps{pbk[1]}"), after=[otail],
                         extra=bo_bc, extra_key="bobc", tag=f"r{b}_")


def emit_ternarize(P, c, wT_dram, rows, cols, out_dram, bounce, bounce128, tag):
    nchunk = rows // 128
    acc = c.ST[:, 192:192 + nchunk]
    for r in range(nchunk):
        P.op("sync", lambda e, r=r: e.dma_start(c.FW[:, 0:cols], wT_dram[128*r:128*(r+1), :]),
             writes=[f"{tag}stg"], dma=True, tag=f"{tag}ld{r}")
        P.op("vector", lambda e, r=r: e.tensor_reduce(acc[:, r:r+1], c.FW[:, 0:cols], AX,
                                                      ALU.add, apply_absolute_value=True),
             reads=[f"{tag}stg"], writes=[f"{tag}acc"], tag=f"{tag}abs{r}")
    tot = c.ST[:, 224:225]
    P.op("vector", lambda e: e.reduce_sum(tot, acc[:, 0:nchunk], AX),
         reads=[f"{tag}acc"], writes=[f"{tag}tot"], tag=f"{tag}sum")
    P.op("sync", lambda e: e.dma_start(bounce128[:, 0:1], tot),
         reads=[f"{tag}tot"], writes=[f"{tag}b128"], dma=True, tag=f"{tag}bout")
    trow = c.ST[0:1, 228 - 128:228] if False else c.ST[0:1, 320:448]
    P.op("sync", lambda e: e.dma_start(trow, bounce128.rearrange("p one -> one p")),
         reads=[f"{tag}b128"], writes=[f"{tag}trow"], dma=True, tag=f"{tag}bin")
    sc = c.ST[:, 225:226]
    P.op("vector", lambda e: e.reduce_sum(sc[0:1, :], trow, AX),
         reads=[f"{tag}trow"], writes=[f"{tag}sc"], tag=f"{tag}scsum")
    P.op("vector", lambda e: e.tensor_scalar(sc[0:1, :], sc[0:1, :], 1.0 / (rows * cols), 1e-8,
                                             ALU.mult, ALU.max),
         reads=[f"{tag}sc"], writes=[f"{tag}sc"], tag=f"{tag}mean")
    P.op("sync", lambda e: e.dma_start(bounce[0:1, 0:1], sc[0:1, :]),
         reads=[f"{tag}sc"], writes=[f"{tag}bnc"], dma=True, tag=f"{tag}scout")
    scbc = c.ST[:, 226:227]
    P.op("sync", lambda e: e.dma_start(scbc, bounce[0:1, 0:1].broadcast_to([128, 1])),
         reads=[f"{tag}bnc"], writes=[f"{tag}scbc"], dma=True, tag=f"{tag}scin")
    P.op("vector", lambda e: e.reciprocal(scbc, scbc),
         reads=[f"{tag}scbc"], writes=[f"{tag}scbc"], tag=f"{tag}rcp")
    for r in range(nchunk):
        P.op("sync", lambda e, r=r: e.dma_start(c.FW[:, 0:cols], wT_dram[128*r:128*(r+1), :]),
             writes=[f"{tag}stg"], dma=True, tag=f"{tag}ld2_{r}")
        t = c.FX[:, 0:cols]
        P.op("vector", lambda e: e.tensor_scalar(t, c.FW[:, 0:cols], scbc, None, ALU.mult),
             reads=[f"{tag}stg", f"{tag}scbc"], writes=[f"{tag}t"], tag=f"{tag}div{r}")
        sgn = c.WB[:, 0:cols]
        P.op("scalar", lambda e: e.activation(sgn, t, AF.Sign),
             reads=[f"{tag}t"], writes=[f"{tag}sgn"], tag=f"{tag}sgn{r}")
        ab = c.FW[:, 0:cols]
        P.op("scalar", lambda e: e.activation(ab, t, AF.Abs),
             reads=[f"{tag}t"], writes=[f"{tag}stg"], tag=f"{tag}abs2{r}")
        P.op("vector", lambda e: e.tensor_scalar(ab, ab, 0.5, None, ALU.is_gt),
             reads=[f"{tag}stg"], writes=[f"{tag}stg"], tag=f"{tag}gt{r}")
        tern = c.WB[:, 4096:4096 + cols]
        P.op("vector", lambda e: e.tensor_tensor(tern, ab, sgn, ALU.mult),
             reads=[f"{tag}stg", f"{tag}sgn"], writes=[f"{tag}tern"], tag=f"{tag}tm{r}")
        P.op("sync", lambda e, r=r: e.dma_start(out_dram[128*r:128*(r+1), :], tern),
             reads=[f"{tag}tern"], dma=True, tag=f"{tag}st{r}")

def emit_bias_bcast(P, c, vec_dram, n, dst_f32, key, tag):
    stage = c.FW[0:1, 3072:3072 + n]
    P.op("sync", lambda e: e.dma_start(stage, vec_dram[None, :]),
         writes=[f"{tag}stage"], dma=True, tag=f"{tag}ld")
    bbf = c.WB[0:1, 9216:9216 + n]
    P.op("vector", lambda e: e.tensor_copy(bbf, stage), reads=[f"{tag}stage"],
         writes=[f"{tag}bf"], tag=f"{tag}cast")
    def mm2(e):
        e.matmul(c.ps[5][:, 0:512], c.ones[0:1, :], bbf[:, 0:512], start=True, stop=True)
        if n > 512:
            e.matmul(c.ps[3][:, 0:n - 512], c.ones[0:1, :], bbf[:, 512:n], start=True, stop=True)
        return tail_fence(e, c)
    P.op("tensor", mm2, reads=[f"{tag}bf", "ones", "ident"],
         writes=["ps5", "ps3", "psfence"], tag=f"{tag}mm")
    P.op("vector", lambda e: e.tensor_copy(dst_f32[:, 0:min(512, n)], c.ps[5][:, 0:min(512, n)]),
         reads=["ps5"], writes=[key], tag=f"{tag}ev0")
    if n > 512:
        P.op("vector", lambda e: e.tensor_copy(dst_f32[:, 512:n], c.ps[3][:, 0:n - 512]),
             reads=["ps3"], writes=[key], tag=f"{tag}ev1")

def emit_L0_ffn(P, c, x_dram, out_dram, W, scratch, prep=True):
    w1t, w2t, hbuf = scratch["w1t"], scratch["w2t"], scratch["hbuf"]
    if prep:
        emit_ternarize(P, c, W["W1T"], 1024, 4096, w1t, scratch["bounce"],
                       scratch["bounce128"], tag="tz1")
        P.barrier()
        emit_ternarize(P, c, W["W2T"], 4096, 1024, w2t, scratch["bounce"],
                       scratch["bounce128"], tag="tz2")
        P.barrier()
    P.op("sync", lambda e: e.dma_start(c.ST[:, 228:260], W["b1r"]),
         writes=["b1r"], dma=True, tag="ldb1")
    for b in range(B):
        P.barrier()
        emit_ln_b(P, c, x_dram[S*b:S*(b+1), :], tag=f"F{b}")
        pend = None
        def ldw1(fc, b=b):
            wslot = 1024 * (fc % 2)
            P.op("sync", lambda e, fc=fc, wslot=wslot: e.dma_start(
                    c.WB[:, wslot:wslot + 1024].rearrange("p (r q) -> p r q", r=8),
                    w1t.rearrange("(r p) f -> p r f", p=128)[:, :, 128*fc:128*(fc+1)]),
                 writes=[f"w1s{fc%2}"], dma=True, tag=f"ldw1_{b}_{fc}")
        ldw1(0)
        for k in range(128):
            fc, n4 = divmod(k, 4)
            bank = k % 4
            wslot = 1024 * (fc % 2)
            if n4 == 0 and fc < 31:
                ldw1(fc + 1)
            def mmh(e, fc=fc, n4=n4, bank=bank, wslot=wslot):
                for r in range(8):
                    e.matmul(c.ps[bank][:], c.WB[:, wslot + 128*r:wslot + 128*(r+1)],
                             c.XLNF[:, 2048*r + 512*n4:2048*r + 512*(n4+1)],
                             start=(r == 0), stop=(r == 7))
                return pe_fence(e, c)
            mmop = P.op("tensor", mmh, reads=[f"w1s{fc%2}", "xlnfm", "ident"],
                        writes=[f"ps{bank}", "psfence"], tag=f"hmm{b}_{fc}_{n4}")
            if pend is not None:
                emit_h_evac(P, c, b, pend, mmop, hbuf)
            pend = (fc, n4, bank)
        tail = P.op("tensor", lambda e: tail_fence(e, c), reads=["ident"],
                    writes=["psfence"], tag=f"htail{b}")
        emit_h_evac(P, c, b, pend, tail, hbuf)
    P.barrier()
    for fc in range(32):
        dst = c.B1 if fc < 16 else c.B2
        P.op("sync", lambda e, fc=fc, dst=dst: e.dma_start(
                dst[:, 1024*(fc % 16):1024*(fc % 16 + 1)], w2t[128*fc:128*(fc+1), :]),
             writes=["w2res"], dma=True, tag=f"ldw2_{fc}")
    b2bc = c.FW[:, 2048:3072]
    emit_bias_bcast(P, c, W["b2"], 1024, b2bc, "b2bc", tag="b2")
    for b in range(B):
        P.barrier()
        xb = x_dram[S*b:S*(b+1), :]
        ob = out_dram[S*b:S*(b+1), :]
        pend = None
        def ldh(tt, b=b):
            hslot = 4096 * (tt % 2)
            P.op("sync", lambda e, tt=tt, b=b, hslot=hslot: e.dma_start(
                    c.B3[:, hslot:hslot + 4096].rearrange("p (f q) -> p f q", f=32),
                    hbuf[4096*b:4096*(b+1), :].rearrange("(f p) t -> p f t", p=128)[:, :, 128*tt:128*(tt+1)]),
                 writes=[f"hs{tt%2}"], dma=True, tag=f"ldh{b}_{tt}")
        ldh(0)
        for tt in range(NT):
            if tt < NT - 1:
                ldh(tt + 1)
            hslot = 4096 * (tt % 2)
            banks = ((2*tt) % 4, (2*tt + 1) % 4)
            def mmd(e, tt=tt, banks=banks, hslot=hslot):
                for h in range(2):
                    for fc in range(32):
                        src = c.B1 if fc < 16 else c.B2
                        e.matmul(c.ps[banks[h]][:], c.B3[:, hslot + 128*fc:hslot + 128*(fc+1)],
                                 src[:, 1024*(fc % 16) + 512*h:1024*(fc % 16) + 512*h + 512],
                                 start=(fc == 0), stop=(fc == 31))
                return pe_fence(e, c)
            mmop = P.op("tensor", mmd, reads=[f"hs{tt%2}", "w2res", "ident"],
                        writes=[f"ps{banks[0]}", f"ps{banks[1]}", "psfence"], tag=f"dmm{b}_{tt}")
            if pend is not None:
                pi, pbk = pend
                emit_resid_store(P, c, xb, ob, pi, (c.ps[pbk[0]][:], c.ps[pbk[1]][:]),
                                 (f"ps{pbk[0]}", f"ps{pbk[1]}"), after=[mmop],
                                 extra=b2bc, extra_key="b2bc", tag=f"fr{b}_")
            pend = (tt, banks)
        tail = P.op("tensor", lambda e: tail_fence(e, c), reads=["ident"],
                    writes=["psfence"], tag=f"dtail{b}")
        pi, pbk = pend
        emit_resid_store(P, c, xb, ob, pi, (c.ps[pbk[0]][:], c.ps[pbk[1]][:]),
                         (f"ps{pbk[0]}", f"ps{pbk[1]}"), after=[tail],
                         extra=b2bc, extra_key="b2bc", tag=f"fr{b}_")

def emit_h_evac(P, c, b, pend, after_op, hbuf):
    fc, n4, bank = pend
    slot = 512 * ((4*fc + n4) % 8)
    skey = f"hslot{(4*fc + n4) % 8}"
    P.op("scalar", lambda e, fc=fc, slot=slot, bank=bank: e.activation(
            c.B3[:, slot:slot + 512], c.ps[bank][:], AF.Gelu,
            bias=c.ST[:, 228 + fc:229 + fc]),
         reads=[f"ps{bank}", "b1r"], writes=[skey], after=[after_op], tag=f"hev{b}_{fc}_{n4}")
    P.op("sync", lambda e, fc=fc, n4=n4, b=b, slot=slot: e.dma_start(
            hbuf[4096*b + 128*fc:4096*b + 128*(fc+1), 512*n4:512*(n4+1)],
            c.B3[:, slot:slot + 512]),
         reads=[skey], dma=True, tag=f"sth{b}_{fc}_{n4}")

def emit_bd2_terms(P, c, b, half_e, colx, b2sq, b2cx):
    for j in range(4):
        e_i = 4*half_e + j
        def dsq(e, j=j):
            return e.tensor_tensor(c.FW[0:1, 1024:2048], c.FX[0:1, 1024*j:1024*(j+1)],
                                   c.FX[0:1, 1024*j:1024*(j+1)], ALU.mult)
        P.op("vector", dsq, reads=["bd2row"], writes=["dsq"], tag=f"lf_sq{b}_{e_i}")
        P.op("vector", lambda e, e_i=e_i: e.tensor_reduce(
                b2sq[:, e_i:e_i+1], c.FW[0:1, 1024:2048], AX, ALU.add),
             reads=["dsq"], writes=["b2sq"], tag=f"lf_sq2{b}_{e_i}")
        def dcx(e, j=j):
            return e.tensor_tensor(c.FW[0:1, 1024:2048], c.FX[0:1, 1024*j:1024*(j+1)],
                                   colx, ALU.mult)
        P.op("vector", dcx, reads=["bd2row", "colx"], writes=["dsq"], tag=f"lf_cx2{b}_{e_i}")
        P.op("vector", lambda e, e_i=e_i: e.tensor_reduce(
                b2cx[:, e_i:e_i+1], c.FW[0:1, 1024:2048], AX, ALU.add),
             reads=["dsq"], writes=["b2cx"], tag=f"lf_cx3{b}_{e_i}")

def emit_L1_ffn(P, c, x_dram, out_dram, W, scratch):
    nc = c.nc
    bounce = scratch["bounce_row"]
    bounce128 = scratch["bounce_pt"]
    T, Dm = S, D
    P.op("sync", lambda e: e.dma_start(c.ST[:, 256:320], W["bd2c"]),
         writes=["bd2c"], dma=True, tag="lf_bd2c")
    bd2c_bf = c.WB[:, 8192:8256]
    P.op("vector", lambda e: e.tensor_copy(bd2c_bf, c.ST[:, 256:320]),
         reads=["bd2c"], writes=["bd2cbf"], tag="lf_bd2cc")
    P.op("sync", lambda e: e.dma_start(c.ST[:, 228:244], W["bd1r"]),
         writes=["b1r"], dma=True, tag="lf_bd1")

    for e_i in range(8):
        for r in range(8):
            stage = c.FW[:, 2048 + 256*(r % 2):2048 + 256*(r % 2) + 256]
            P.op("sync", lambda e, e_i=e_i, r=r, stage=stage: e.dma_start(
                    stage, W["Wd2T"][e_i, 128*r:128*(r+1), :]),
                 writes=[f"w2s{r%2}"], dma=True, tag=f"lf_w2t{e_i}_{r}")
            P.op("vector", lambda e, r=r, stage=stage: e.tensor_copy(
                    c.WB[:, 256*r:256*(r+1)], stage),
                 reads=[f"w2s{r%2}"], writes=["w2tT"], tag=f"lf_w2tc{e_i}_{r}")
        def mmw(e, e_i=e_i):
            for hc in range(2):
                for r in range(8):
                    e.matmul(c.ps[hc][:, 0:256], c.WB[:, 256*r + 128*hc:256*r + 128*hc + 128],
                             c.WB[:, 256*r:256*(r+1)], start=(r == 0), stop=(r == 7))
            for hc in range(2):
                for r in range(8):
                    e.matmul(c.ps[2 + hc][:, 0:1], c.WB[:, 256*r + 128*hc:256*r + 128*hc + 128],
                             bd2c_bf[:, 8*e_i + r:8*e_i + r + 1], start=(r == 0), stop=(r == 7))
            return pe_fence(e, c)
        mmop = P.op("tensor", mmw, reads=["w2tT", "bd2cbf", "ident"],
                    writes=["ps0", "ps1", "ps2", "ps3", "psfence"], tag=f"lf_mmw{e_i}")
        tailop = P.op("tensor", lambda e: tail_fence(e, c), reads=["ident"],
                      writes=["psfence"], tag=f"lf_mmwt{e_i}")
        for hc in range(2):
            P.op("vector", lambda e, e_i=e_i, hc=hc: e.tensor_copy(
                    c.B2[:, 2048*hc + 256*e_i:2048*hc + 256*(e_i+1)], c.ps[hc][:, 0:256]),
                 reads=[f"ps{hc}"], writes=["w2w2t"], after=[tailop], tag=f"lf_w2ev{e_i}_{hc}")
            P.op("vector", lambda e, e_i=e_i, hc=hc: e.tensor_copy(
                    c.ST[:, 448 + 2*e_i + hc:449 + 2*e_i + hc], c.ps[2 + hc][:, 0:1]),
                 reads=[f"ps{2+hc}"], writes=["w2b2"], after=[tailop], tag=f"lf_wbev{e_i}_{hc}")
    for b in range(B):
        P.barrier()
        xb = x_dram[S*b:S*(b+1), :]
        ob = out_dram[S*b:S*(b+1), :]
        emit_ln_b(P, c, xb, tag=f"E{b}", var_stash=c.ST[:, 200:216],
                  dst_tm=c.B1, dst_tm_key="xlntm")
        PT = c.ST[:, 64:96]
        P.op("vector", lambda e: e.memset(PT[:, 0:32], 0.0), writes=["PT"], tag=f"lf_pt0_{b}")
        def mmcx(e):
            for h in range(2):
                for i in range(NT):
                    e.matmul(c.ps[0][0:1, 0:512] if h == 0 else c.ps[1][0:1, 0:512],
                             c.ones[:, 0:1], c.B1[:, 1024*i + 512*h:1024*i + 512*(h+1)],
                             start=(i == 0), stop=(i == NT - 1))
            return pe_fence(e, c)
        cxop = P.op("tensor", mmcx, reads=["xlntm", "ones", "ident"],
                    writes=["ps0", "ps1", "psfence"], tag=f"lf_cx{b}")
        cxtail = P.op("tensor", lambda e: tail_fence(e, c), reads=["ident"],
                      writes=["psfence"], tag=f"lf_cxt{b}")
        colx = c.FW[0:1, 0:1024]
        P.op("vector", lambda e: e.tensor_copy(colx[:, 0:512], c.ps[0][0:1, 0:512]),
             reads=["ps0"], writes=["colx"], after=[cxtail], tag=f"lf_cxev0_{b}")
        P.op("vector", lambda e: e.tensor_copy(colx[:, 512:1024], c.ps[1][0:1, 0:512]),
             reads=["ps1"], writes=["colx"], after=[cxtail], tag=f"lf_cxev1_{b}")
        vs = c.ST[:, 200:216]
        tmp = c.ST[:, 216:217]
        P.op("vector", lambda e: e.tensor_scalar(c.ST[:, 96:112], vs, 1e-5, None, ALU.add),
             reads=["varstash"], writes=["vtmp"], tag=f"lf_v1_{b}")
        P.op("vector", lambda e: e.reciprocal(c.ST[:, 96:112], c.ST[:, 96:112]),
             reads=["vtmp"], writes=["vtmp"], tag=f"lf_v2_{b}")
        P.op("vector", lambda e: e.tensor_tensor(c.ST[:, 96:112], c.ST[:, 96:112], vs, ALU.mult),
             reads=["vtmp", "varstash"], writes=["vtmp"], tag=f"lf_v3_{b}")
        P.op("vector", lambda e: e.tensor_reduce(tmp, c.ST[:, 96:112], AX, ALU.add),
             reads=["vtmp"], writes=["xxp"], tag=f"lf_v4_{b}")
        P.op("vector", lambda e: e.tensor_scalar(PT[:, 24:25], tmp, 1024.0, None, ALU.mult),
             reads=["xxp"], writes=["PT"], tag=f"lf_v5_{b}")
        for e_i in range(8):
            for r in range(8):
                stage = c.FW[:, 2048 + 256*(r % 2):2048 + 256*(r % 2) + 256]
                P.op("sync", lambda e, e_i=e_i, r=r, stage=stage: e.dma_start(
                        stage, W["Wd1"][e_i, 128*r:128*(r+1), :]),
                     writes=[f"w2s{r%2}"], dma=True, tag=f"lf_wd1{b}_{e_i}_{r}")
                P.op("vector", lambda e, r=r, stage=stage: e.tensor_copy(
                        c.WB[:, 4096 + 256*r:4096 + 256*(r+1)], stage),
                     reads=[f"w2s{r%2}"], writes=["wd1"], tag=f"lf_wd1c{b}_{e_i}_{r}")
            for hc in range(2):
                stage = c.FW[:, 2048:3072] if False else c.FW[:, 3072:4096]
                P.op("sync", lambda e, e_i=e_i, hc=hc, stage=stage: e.dma_start(
                        stage, W["Wd2n"][e_i, 128*hc:128*(hc+1), :]),
                     writes=["w2ns"], dma=True, tag=f"lf_wd2n{b}_{e_i}_{hc}")
                P.op("vector", lambda e, hc=hc, stage=stage: e.tensor_copy(
                        c.WB[:, 6144 + 1024*hc:6144 + 1024*(hc+1)], stage),
                     reads=["w2ns"], writes=["wd2n"], tag=f"lf_wd2nc{b}_{e_i}_{hc}")
            pend = None
            for k in range(8):
                hc, n4 = divmod(k, 4)
                bank = k % 4
                def mmh(e, hc=hc, n4=n4, bank=bank):
                    for r in range(8):
                        e.matmul(c.ps[bank][:],
                                 c.WB[:, 4096 + 256*r + 128*hc:4096 + 256*r + 128*hc + 128],
                                 c.XLNF[:, 2048*r + 512*n4:2048*r + 512*(n4+1)],
                                 start=(r == 0), stop=(r == 7))
                    return pe_fence(e, c)
                mmop = P.op("tensor", mmh, reads=["wd1", "xlnfm", "ident"],
                            writes=[f"ps{bank}", "psfence"], tag=f"lf_h{b}_{e_i}_{k}")
                if pend is not None:
                    phc, pn4, pb = pend
                    P.op("scalar", lambda e, phc=phc, pn4=pn4, pb=pb, e_i=e_i: e.activation(
                            c.B3[:, 2048*phc + 512*pn4:2048*phc + 512*(pn4+1)], c.ps[pb][:],
                            AF.Relu, bias=c.ST[:, 228 + 2*e_i + phc:229 + 2*e_i + phc]),
                         reads=[f"ps{pb}", "b1r"], writes=["hfm"], after=[mmop],
                         tag=f"lf_hev{b}_{e_i}_{pn4}_{phc}")
                pend = (hc, n4, bank)
            tail = P.op("tensor", lambda e: tail_fence(e, c), reads=["ident"],
                        writes=["psfence"], tag=f"lf_ht{b}_{e_i}")
            phc, pn4, pb = pend
            P.op("scalar", lambda e, phc=phc, pn4=pn4, pb=pb, e_i=e_i: e.activation(
                    c.B3[:, 2048*phc + 512*pn4:2048*phc + 512*(pn4+1)], c.ps[pb][:],
                    AF.Relu, bias=c.ST[:, 228 + 2*e_i + phc:229 + 2*e_i + phc]),
                 reads=[f"ps{pb}", "b1r"], writes=["hfm"], after=[tail],
                 tag=f"lf_hev{b}_{e_i}_{pn4}_{phc}")
            pendT = None
            for ti in range(NT):
                def tr2(e, ti=ti):
                    for hc in range(2):
                        e.transpose(c.pt[ti % 2][:, 128*hc:128*(hc+1)],
                                    c.B3[:, 2048*hc + 128*ti:2048*hc + 128*(ti+1)], c.ident)
                    return pe_fence(e, c)
                trop = P.op("tensor", tr2, reads=["hfm", "ident"],
                            writes=[f"pt{ti%2}", "psfence"], tag=f"lf_T{b}_{e_i}_{ti}")
                if pendT is not None:
                    pti = pendT
                    P.op("vector", lambda e, pti=pti: e.tensor_copy(
                            c.WB[:, 256*pti:256*(pti+1)], c.pt[pti % 2][:, 0:256]),
                         reads=[f"pt{pti%2}"], writes=["htm"], after=[trop],
                         tag=f"lf_Tev{b}_{e_i}_{pti}")
                pendT = ti
            tailT = P.op("tensor", lambda e: tail_fence(e, c), reads=["ident"],
                         writes=["psfence"], tag=f"lf_Tt{b}_{e_i}")
            P.op("vector", lambda e, pti=pendT: e.tensor_copy(
                    c.WB[:, 256*pti:256*(pti+1)], c.pt[pti % 2][:, 0:256]),
                 reads=[f"pt{pendT%2}"], writes=["htm"], after=[tailT],
                 tag=f"lf_Tev{b}_{e_i}_{pendT}")
            def mmq(e):
                for hc in range(2):
                    for half in range(2):
                        bank = 2*hc + half
                        for ti in range(NT):
                            e.matmul(c.ps[bank][:],
                                     c.WB[:, 256*ti + 128*hc:256*ti + 128*hc + 128],
                                     c.B1[:, 1024*ti + 512*half:1024*ti + 512*(half+1)],
                                     start=(ti == 0), stop=(ti == NT - 1))
                return pe_fence(e, c)
            q1 = P.op("tensor", mmq, reads=["htm", "xlntm", "ident"],
                      writes=["ps0", "ps1", "ps2", "ps3", "psfence"], tag=f"lf_q1{b}_{e_i}")
            def mmg(e):
                for hc in range(2):
                    for ti in range(NT):
                        e.matmul(c.ps[4][:, 0:256] if hc == 0 else c.ps[5][:, 0:256],
                                 c.WB[:, 256*ti + 128*hc:256*ti + 128*hc + 128],
                                 c.WB[:, 256*ti:256*(ti+1)],
                                 start=(ti == 0), stop=(ti == NT - 1))
                for hc in range(2):
                    for ti in range(NT):
                        e.matmul(c.ps[4][:, 256 + hc:257 + hc],
                                 c.WB[:, 256*ti + 128*hc:256*ti + 128*hc + 128],
                                 c.ones[:, 0:1], start=(ti == 0), stop=(ti == NT - 1))
                return pe_fence(e, c)
            q2 = P.op("tensor", mmg, reads=["htm", "ones", "ident"],
                      writes=["ps4", "ps5", "psfence"], tag=f"lf_q2{b}_{e_i}")
            qt = P.op("tensor", lambda e: tail_fence(e, c), reads=["ident"],
                      writes=["psfence"], tag=f"lf_qt{b}_{e_i}")
            sc1 = c.ST[:, 112:113]
            for hc in range(2):
                for half in range(2):
                    bank = 2*hc + half
                    P.op("vector", lambda e, hc=hc, half=half, bank=bank: e.tensor_tensor(
                            c.FX[:, 0:512], c.ps[bank][:],
                            c.WB[:, 6144 + 1024*hc + 512*half:6144 + 1024*hc + 512*half + 512],
                            ALU.mult),
                         reads=[f"ps{bank}", "wd2n"], writes=["dotscr"], after=[qt],
                         tag=f"lf_d1{b}_{e_i}_{bank}")
                    P.op("vector", lambda e: e.tensor_reduce(sc1, c.FX[:, 0:512], AX, ALU.add),
                         reads=["dotscr"], writes=["sc1"], tag=f"lf_d2{b}_{e_i}_{bank}")
                    P.op("vector", lambda e, e_i=e_i: e.tensor_tensor(
                            PT[:, 8 + e_i:9 + e_i], PT[:, 8 + e_i:9 + e_i], sc1, ALU.add),
                         reads=["sc1", "PT"], writes=["PT"], tag=f"lf_d3{b}_{e_i}_{bank}")
            for hc in range(2):
                P.op("vector", lambda e, hc=hc, e_i=e_i: e.tensor_tensor(
                        c.FX[:, 0:256], c.ps[4][:, 0:256] if hc == 0 else c.ps[5][:, 0:256],
                        c.B2[:, 2048*hc + 256*e_i:2048*hc + 256*(e_i+1)], ALU.mult),
                     reads=[f"ps{4+hc}", "w2w2t"], writes=["dotscr"], after=[qt],
                     tag=f"lf_d4{b}_{e_i}_{hc}")
                P.op("vector", lambda e: e.tensor_reduce(sc1, c.FX[:, 0:256], AX, ALU.add),
                     reads=["dotscr"], writes=["sc1"], tag=f"lf_d5{b}_{e_i}_{hc}")
                P.op("vector", lambda e, e_i=e_i: e.tensor_tensor(
                        PT[:, e_i:1 + e_i], PT[:, e_i:1 + e_i], sc1, ALU.add),
                     reads=["sc1", "PT"], writes=["PT"], tag=f"lf_d6{b}_{e_i}_{hc}")
            for hc in range(2):
                P.op("vector", lambda e, hc=hc, e_i=e_i: e.tensor_tensor(
                        sc1, c.ps[4][:, 256 + hc:257 + hc],
                        c.ST[:, 448 + 2*e_i + hc:449 + 2*e_i + hc], ALU.mult),
                     reads=["ps4", "w2b2"], writes=["sc1"], after=[qt],
                     tag=f"lf_d7{b}_{e_i}_{hc}")
                P.op("vector", lambda e, e_i=e_i: e.tensor_tensor(
                        PT[:, 16 + e_i:17 + e_i], PT[:, 16 + e_i:17 + e_i], sc1, ALU.add),
                     reads=["sc1", "PT"], writes=["PT"], tag=f"lf_d8{b}_{e_i}_{hc}")
        def trPT(e):
            e.transpose(c.ps[5][0:32, 0:128], PT[:, 0:32], c.IDF[:])
            return tail_fence(e, c)
        trop = P.op("tensor", trPT, reads=["PT", "identf", "ident"],
                    writes=["ps5", "psfence"], tag=f"lf_trpt{b}")
        ptred = c.ST[0:32, 384:385]
        P.op("vector", lambda e: e.tensor_reduce(ptred, c.ps[5][0:32, 0:128], AX, ALU.add),
             reads=["ps5"], writes=["ptred"], after=[trop], tag=f"lf_rr{b}")
        def trPT2(e):
            e.transpose(c.ps[5][0:1, 0:32], ptred, c.IDF[0:32, 0:32])
            return tail_fence(e, c)
        trop2 = P.op("tensor", trPT2, reads=["ptred", "identf", "ident"],
                     writes=["ps5", "psfence"], tag=f"lf_trpt2{b}")
        prow = c.ST[0:1, 464:496]
        P.op("vector", lambda e: e.tensor_copy(prow, c.ps[5][0:1, 0:32]),
             reads=["ps5"], writes=["prow"], after=[trop2], tag=f"lf_prow{b}")
        P.barrier()
        b2sq = c.ST[0:1, 496:504]
        b2cx = c.ST[0:1, 504:512]
        for half_e in range(2):
            P.op("sync", lambda e, half_e=half_e: e.dma_start(
                    c.FX[0:1, 0:4096], W["bd2f2"][half_e:half_e+1, :]),
                 writes=["bd2row"], dma=True, tag=f"lf_bd2row{b}_{half_e}")
            emit_bd2_terms(P, c, b, half_e, colx, b2sq, b2cx)
        dummy_loop_removed = True
        for e_i in []:
            rr, cc0 = divmod(e_i, 4)

        lossr = c.ST[0:1, 217:225]
        P.op("vector", lambda e: e.scalar_tensor_tensor(lossr, prow[:, 8:16], -2.0,
                                                        prow[:, 0:8], ALU.mult, ALU.add),
             reads=["prow"], writes=["lossr"], tag=f"lf_l1{b}")
        P.op("vector", lambda e: e.scalar_tensor_tensor(c.ST[0:1, 120:128], prow[:, 16:24], 2.0,
                                                        lossr, ALU.mult, ALU.add),
             reads=["prow", "lossr"], writes=["lossr2"], tag=f"lf_l2{b}")
        P.op("vector", lambda e: e.tensor_scalar(lossr, c.ST[0:1, 120:128], prow[:, 24:25],
                                                 None, ALU.add),
             reads=["lossr2", "prow"], writes=["lossr"], tag=f"lf_l3{b}")
        P.op("vector", lambda e: e.scalar_tensor_tensor(lossr, b2sq, float(T), lossr,
                                                        ALU.mult, ALU.add),
             reads=["b2sq", "lossr"], writes=["lossr"], tag=f"lf_l4{b}")
        P.op("vector", lambda e: e.scalar_tensor_tensor(lossr, b2cx, -2.0, lossr,
                                                        ALU.mult, ALU.add),
             reads=["b2cx", "lossr"], writes=["lossr"], tag=f"lf_l5{b}")
        mn = c.ST[0:1, 225:226]
        P.op("vector", lambda e: e.tensor_reduce(mn, lossr, AX, ALU.min),
             reads=["lossr"], writes=["mn"], tag=f"lf_mn{b}")
        maskr = c.ST[0:1, 128:136]
        P.op("vector", lambda e: e.tensor_scalar(maskr, lossr, mn, None, ALU.is_le),
             reads=["lossr", "mn"], writes=["maskr"], tag=f"lf_msk{b}")
        if DEBUG_STASH is not None:
            P.op("vector", lambda e, b=b: e.tensor_copy(c.B3[0:1, 16384+16*b:16384+16*b+8],
                                                        lossr),
                 reads=["lossr"], writes=["lossdbg"], tag=f"lf_ldbg{b}")
        P.op("sync", lambda e: e.dma_start(bounce[1:2, 0:8], maskr),
             reads=["maskr"], writes=["bmask"], dma=True, tag=f"lf_mbo{b}")
        maskbc = c.ST[:, 244:252]
        P.op("sync", lambda e: e.dma_start(maskbc, bounce[1:2, 0:8].broadcast_to([128, 8])),
             reads=["bmask"], writes=["maskbc"], dma=True, tag=f"lf_mbi{b}")
        wdacc = c.FX[:, 0:512]
        wuacc = c.FX[0:64, 512:1536]
        for e_i in range(8):
            for r in range(8):
                stage = c.FW[:, 2048 + 64*(r % 2):2048 + 64*(r % 2) + 64]
                P.op("sync", lambda e, e_i=e_i, r=r, stage=stage: e.dma_start(
                        stage, W["Wa_down"][e_i, 128*r:128*(r+1), :]),
                     writes=["wadstage"], dma=True, tag=f"lf_wad{b}_{e_i}_{r}")
                if e_i == 0:
                    P.op("vector", lambda e, r=r, stage=stage: e.tensor_scalar(
                            wdacc[:, 64*r:64*(r+1)], stage, maskbc[:, 0:1], None, ALU.mult),
                         reads=["wadstage", "maskbc"], writes=["wdacc"], tag=f"lf_wda{b}_{e_i}_{r}")
                else:
                    P.op("vector", lambda e, e_i=e_i, r=r, stage=stage: e.scalar_tensor_tensor(
                            wdacc[:, 64*r:64*(r+1)], stage, maskbc[:, e_i:e_i+1],
                            wdacc[:, 64*r:64*(r+1)], ALU.mult, ALU.add),
                         reads=["wadstage", "maskbc", "wdacc"], writes=["wdacc"], tag=f"lf_wda{b}_{e_i}_{r}")
            stage2 = c.FW[0:64, 3072:4096]
            P.op("sync", lambda e, e_i=e_i, stage2=stage2: e.dma_start(
                    stage2, W["Wa_up"][e_i]),
                 writes=["waustage"], dma=True, tag=f"lf_wau{b}_{e_i}")
            if e_i == 0:
                P.op("vector", lambda e, stage2=stage2: e.tensor_scalar(
                        wuacc, stage2, maskbc[0:64, 0:1], None, ALU.mult),
                     reads=["waustage", "maskbc"], writes=["wuacc"], tag=f"lf_wua{b}_{e_i}")
            else:
                P.op("vector", lambda e, e_i=e_i, stage2=stage2: e.scalar_tensor_tensor(
                        wuacc, stage2, maskbc[0:64, e_i:e_i+1], wuacc, ALU.mult, ALU.add),
                     reads=["waustage", "maskbc", "wuacc"], writes=["wuacc"], tag=f"lf_wua{b}_{e_i}")
        wdbf = c.WB[:, 0:512]
        wubf = c.WB[0:64, 512:1536]
        P.op("vector", lambda e: e.tensor_copy(wdbf, wdacc), reads=["wdacc"],
             writes=["wdbf"], tag=f"lf_wdc{b}")
        P.op("vector", lambda e: e.tensor_copy(wubf, wuacc), reads=["wuacc"],
             writes=["wubf"], tag=f"lf_wuc{b}")
        bad = c.ST[0:1, 320:384]
        bau = c.FW[0:1, 1024:2048]
        P.op("sync", lambda e: e.dma_start(c.FW[0:1, 2560:3072], W["ba_downf"]),
             writes=["badf"], dma=True, tag=f"lf_bad{b}")
        P.op("sync", lambda e: e.dma_start(c.FW[0:1, 0:1024], W["b_base"][None, :]),
             writes=["bbase"], dma=True, tag=f"lf_bb{b}")
        for e_i in range(8):
            if e_i == 0:
                P.op("vector", lambda e: e.tensor_scalar(bad, c.FW[0:1, 2560:2624],
                                                         maskr[:, 0:1], None, ALU.mult),
                     reads=["badf", "maskr"], writes=["bad"], tag=f"lf_bds{b}_{e_i}")
            else:
                P.op("vector", lambda e, e_i=e_i: e.scalar_tensor_tensor(
                        bad, c.FW[0:1, 2560 + 64*e_i:2624 + 64*e_i], maskr[:, e_i:e_i+1],
                        bad, ALU.mult, ALU.add),
                     reads=["badf", "maskr", "bad"], writes=["bad"], tag=f"lf_bds{b}_{e_i}")
        P.op("vector", lambda e: e.tensor_copy(bau, c.FW[0:1, 0:1024]),
             reads=["bbase"], writes=["bau"], tag=f"lf_bau0{b}")
        for e_i in range(8):
            P.op("sync", lambda e, e_i=e_i: e.dma_start(c.FW[0:1, 0:1024], W["ba_upf"][0:1, 1024*e_i:1024*(e_i+1)]),
                 writes=["bbase"], dma=True, tag=f"lf_bau{b}_{e_i}")
            P.op("vector", lambda e, e_i=e_i: e.scalar_tensor_tensor(
                    bau, c.FW[0:1, 0:1024], maskr[:, e_i:e_i+1], bau, ALU.mult, ALU.add),
                 reads=["bbase", "maskr", "bau"], writes=["bau"], tag=f"lf_baua{b}_{e_i}")
        P.op("sync", lambda e: e.dma_start(bounce[0:1, 0:64], bad),
             reads=["bad"], writes=["bbad"], dma=True, tag=f"lf_bad2{b}")
        badp = c.ST[0:64, 136:137]
        P.op("sync", lambda e: e.dma_start(badp, bounce[0:1, 0:64].rearrange("one p -> p one")),
             reads=["bbad"], writes=["badp"], dma=True, tag=f"lf_bad3{b}")
        P.op("sync", lambda e: e.dma_start(bounce[0:1, 0:1024], bau),
             reads=["bau"], writes=["bbau"], dma=True, tag=f"lf_bau2{b}")
        biasbc = c.FX[:, 0:1024]
        P.op("sync", lambda e: e.dma_start(biasbc, bounce[0:1, 0:1024].broadcast_to([128, 1024])),
             reads=["bbau"], writes=["biasbc", "wdacc", "wuacc"], dma=True, tag=f"lf_bau3{b}")
        P.barrier()
        pend = None
        for n4 in range(4):
            def mma(e, n4=n4):
                for r in range(8):
                    e.matmul(c.ps[n4 % 4][0:64, :], wdbf[:, 64*r:64*(r+1)],
                             c.XLNF[:, 2048*r + 512*n4:2048*r + 512*(n4+1)],
                             start=(r == 0), stop=(r == 7))
                return pe_fence(e, c)
            mmop = P.op("tensor", mma, reads=["wdbf", "xlnfm", "ident"],
                        writes=[f"ps{n4%4}", "psfence"], tag=f"lf_a{b}_{n4}")
            if pend is not None:
                pn4 = pend
                P.op("scalar", lambda e, pn4=pn4: e.activation(
                        c.B3[0:64, 4096 + 512*pn4:4096 + 512*(pn4+1)], c.ps[pn4 % 4][0:64, :],
                        AF.Relu, bias=badp),
                     reads=[f"ps{pn4%4}", "badp"], writes=["afm"], after=[mmop],
                     tag=f"lf_aev{b}_{pn4}")
            pend = n4
        tailA = P.op("tensor", lambda e: tail_fence(e, c), reads=["ident"],
                     writes=["psfence"], tag=f"lf_at{b}")
        P.op("scalar", lambda e, pn4=pend: e.activation(
                c.B3[0:64, 4096 + 512*pn4:4096 + 512*(pn4+1)], c.ps[pn4 % 4][0:64, :],
                AF.Relu, bias=badp),
             reads=[f"ps{pend%4}", "badp"], writes=["afm"], after=[tailA],
             tag=f"lf_aev{b}_{pend}")
        for r in range(8):
            stage = c.FW[:, 3072 + 1024*(r % 1):4096] if False else c.FW[:, 3072:4096]
            P.op("sync", lambda e, r=r, stage=stage: e.dma_start(stage, W["W_base"][128*r:128*(r+1), :]),
                 writes=["wbstage"], dma=True, tag=f"lf_wb{b}_{r}")
            P.op("vector", lambda e, r=r, stage=stage: e.tensor_copy(
                    c.B3[:, 8192 + 1024*r:8192 + 1024*(r+1)], stage),
                 reads=["wbstage"], writes=["wbase"], tag=f"lf_wbc{b}_{r}")
        pend = None
        for tt in range(NT):
            banks = ((2*tt) % 4, (2*tt + 1) % 4)
            def mmd(e, tt=tt, banks=banks):
                for h in range(2):
                    for r in range(8):
                        e.matmul(c.ps[banks[h]][:],
                                 c.XLNF[:, 2048*r + 128*tt:2048*r + 128*(tt+1)],
                                 c.B3[:, 8192 + 1024*r + 512*h:8192 + 1024*r + 512*h + 512],
                                 start=(r == 0), stop=False)
                    e.matmul(c.ps[banks[h]][:], c.B3[0:64, 4096 + 128*tt:4096 + 128*(tt+1)],
                             wubf[:, 512*h:512*(h+1)], start=False, stop=True)
                return pe_fence(e, c)
            mmop = P.op("tensor", mmd, reads=["xlnfm", "wbase", "afm", "wubf", "ident"],
                        writes=[f"ps{banks[0]}", f"ps{banks[1]}", "psfence"], tag=f"lf_dm{b}_{tt}")
            if pend is not None:
                pi, pbk = pend
                emit_resid_store(P, c, xb, ob, pi, (c.ps[pbk[0]][:], c.ps[pbk[1]][:]),
                                 (f"ps{pbk[0]}", f"ps{pbk[1]}"), after=[mmop],
                                 extra=biasbc, extra_key="biasbc", tag=f"lr{b}_")
            pend = (tt, banks)
        tailD = P.op("tensor", lambda e: tail_fence(e, c), reads=["ident"],
                     writes=["psfence"], tag=f"lf_dt{b}")
        pi, pbk = pend
        emit_resid_store(P, c, xb, ob, pi, (c.ps[pbk[0]][:], c.ps[pbk[1]][:]),
                         (f"ps{pbk[0]}", f"ps{pbk[1]}"), after=[tailD],
                         extra=biasbc, extra_key="biasbc", tag=f"lr{b}_")

def emit_elu1_evac(P, c, ps_ap, ps_key, dst_ap, dst_key, after_op, tag):
    n = dst_ap.shape[1]
    s1 = c.FX[0:dst_ap.shape[0], 2048:2048 + n]
    s2 = c.FX[0:dst_ap.shape[0], 3072:3072 + n]
    P.op("vector", lambda e: e.tensor_scalar(s1, ps_ap, 0.0, None, ALU.min),
         reads=[ps_key], writes=[f"{tag}s1"], after=[after_op], tag=f"{tag}min")
    P.op("scalar", lambda e: e.activation(s1, s1, AF.Exp),
         reads=[f"{tag}s1"], writes=[f"{tag}s1"], tag=f"{tag}exp")
    P.op("vector", lambda e: e.tensor_scalar(s2, ps_ap, 0.0, None, ALU.max),
         reads=[ps_key], writes=[f"{tag}s2"], after=[after_op], tag=f"{tag}relu")
    P.op("vector", lambda e: e.tensor_add(dst_ap, s1, s2),
         reads=[f"{tag}s1", f"{tag}s2"], writes=[dst_key], tag=f"{tag}add")

def emit_L0_attn_a(P, c, x_dram, out_dram, W):
    SCALE = 64.0 ** -0.5
    P.op("sync", lambda e: e.dma_start(c.FW[0:1, 0:1024], W["bg"][None, :]),
         writes=["bgstage"], dma=True, tag="aa_bg")
    bg_bf = c.WB[0:1, 10200:11224] if False else None
    bg_bf = c.WB[0:1, 9216:10240]
    P.op("vector", lambda e: e.tensor_copy(bg_bf, c.FW[0:1, 0:1024]),
         reads=["bgstage"], writes=["bgbf"], tag="aa_bgc")
    for b in range(B):
        P.barrier()
        xb = x_dram[S*b:S*(b+1), :]
        ob = out_dram[S*b:S*(b+1), :]
        emit_ln_b(P, c, xb, tag=f"G{b}")
        for name, wname in (("k", "Wk"), ("v", "Wv"), ("g", "Wg")):
            emit_load_w1024(P, c, W[wname], tag=f"aa_{name}w{b}")
            pend = None
            for i in range(NT):
                banks = ((2*i) % 4, (2*i + 1) % 4)
                def mm(e, i=i, banks=banks, name=name):
                    for h in range(2):
                        for r in range(8):
                            e.matmul(c.ps[banks[h]][:],
                                     c.XLNF[:, 2048*r + 128*i:2048*r + 128*(i+1)],
                                     c.WB[:, 1024*r + 512*h:1024*r + 512*(h+1)],
                                     start=(r == 0), stop=(False if name == "g" else r == 7))
                        if name == "g":
                            e.matmul(c.ps[banks[h]][:], c.ones[0:1, 0:128],
                                     bg_bf[:, 512*h:512*(h+1)], start=False, stop=True)
                    return pe_fence(e, c)
                mmop = P.op("tensor", mm,
                            reads=["xlnfm", "wcur", "ident"] + (["bgbf"] if name == "g" else []),
                            writes=[f"ps{banks[0]}", f"ps{banks[1]}", "psfence"],
                            tag=f"aa_{name}mm{b}_{i}")
                if pend is not None:
                    emit_kvg_evac(P, c, b, name, pend, mmop)
                pend = (i, banks)
            tail = P.op("tensor", lambda e: tail_fence(e, c), reads=["ident"],
                        writes=["psfence"], tag=f"aa_{name}t{b}")
            emit_kvg_evac(P, c, b, name, pend, tail)
        P.barrier()
        qbuf = c.WB[0:64, 0:2048]
        kvsb = c.WB[0:64, 2048:2113]
        for hh in range(16):
            emit_load_w1024_cols(P, c, W["Wq"], 64*hh, 64, tag=f"aa_qw{b}_{hh}")
            pend = None
            for n4 in range(4):
                def mmq(e, n4=n4, hh=hh):
                    for r in range(8):
                        e.matmul(c.ps[n4 % 4][0:64, :], c.WB[:, 2176 + 64*r:2176 + 64*(r+1)],
                                 c.XLNF[:, 2048*r + 512*n4:2048*r + 512*(n4+1)],
                                 start=(r == 0), stop=(r == 7))
                    return pe_fence(e, c)
                mmop = P.op("tensor", mmq, reads=["qwcur", "xlnfm", "ident"],
                            writes=[f"ps{n4%4}", "psfence"], tag=f"aa_q{b}_{hh}_{n4}")
                if pend is not None:
                    pn4 = pend
                    emit_elu1_evac(P, c, c.ps[pn4 % 4][0:64, :], f"ps{pn4%4}",
                                   qbuf[:, 512*pn4:512*(pn4+1)], "qbuf", mmop,
                                   tag=f"aa_qe{b}_{hh}_{pn4}_")
                pend = n4
            def mmkv(e, hh=hh):
                for i in range(NT):
                    e.matmul(c.ps[4][0:64, 0:64], c.B1[:, 1024*i + 64*hh:1024*i + 64*(hh+1)],
                             c.B2[:, 1024*i + 64*hh:1024*i + 64*(hh+1)],
                             start=(i == 0), stop=(i == NT - 1))
                for i in range(NT):
                    e.matmul(c.ps[4][0:64, 64:65], c.B1[:, 1024*i + 64*hh:1024*i + 64*(hh+1)],
                             c.ones[:, 0:1], start=(i == 0), stop=(i == NT - 1))
                return pe_fence(e, c)
            kvop = P.op("tensor", mmkv, reads=["ktm", "vtm", "ones", "ident"],
                        writes=["ps4d", "psfence"], tag=f"aa_kv{b}_{hh}")
            ktail = P.op("tensor", lambda e: tail_fence(e, c), reads=["ident"],
                         writes=["psfence"], tag=f"aa_kvt{b}_{hh}")
            P.op("vector", lambda e: e.tensor_copy(kvsb, c.ps[4][0:64, 0:65]),
                 reads=["ps4d"], writes=["kvsb"], after=[ktail], tag=f"aa_kve{b}_{hh}")
            emit_elu1_evac(P, c, c.ps[pend % 4][0:64, :], f"ps{pend%4}",
                           qbuf[:, 512*pend:512*(pend+1)], "qbuf", ktail,
                           tag=f"aa_qe{b}_{hh}_{pend}_")
            pend2 = None
            for i in range(NT):
                bank = i % 4
                def mmo(e, i=i, bank=bank):
                    e.matmul(c.ps[bank][:, 0:65], qbuf[:, 128*i:128*(i+1)], kvsb,
                             start=True, stop=True)
                    return pe_fence(e, c)
                oop = P.op("tensor", mmo, reads=["qbuf", "kvsb", "ident"],
                           writes=[f"ps{bank}", "psfence"], tag=f"aa_o{b}_{hh}_{i}")
                if pend2 is not None:
                    emit_gla_out_evac(P, c, b, hh, pend2, oop, SCALE)
                pend2 = (i, bank)
            otail = P.op("tensor", lambda e: tail_fence(e, c), reads=["ident"],
                         writes=["psfence"], tag=f"aa_ot{b}_{hh}")
            emit_gla_out_evac(P, c, b, hh, pend2, otail, SCALE)
        P.barrier()
        pendT = None
        for i in range(NT):
            def t8(e, i=i):
                for j in range(8):
                    e.transpose(c.pt[i % 2][:, 128*j:128*(j+1)],
                                c.B3[:, 1024*i + 128*j:1024*i + 128*(j+1)], c.ident)
                return pe_fence(e, c)
            top = P.op("tensor", t8, reads=["gout", "ident"], writes=[f"pt{i%2}", "psfence"],
                       tag=f"aa_T{b}_{i}")
            if pendT is not None:
                pi = pendT
                def ev(e, pi=pi):
                    src = c.pt[pi % 2][:].rearrange("p (j t) -> p j t", j=8)
                    dstap = c.XLNF[:].rearrange("p (j s) -> p j s", j=8)[:, :, 128*pi:128*(pi+1)]
                    return e.tensor_copy(dstap, src)
                P.op("vector", ev, reads=[f"pt{pi%2}"], writes=["outfm"], after=[top],
                     tag=f"aa_Tev{b}_{pi}")
            pendT = i
        tailT = P.op("tensor", lambda e: tail_fence(e, c), reads=["ident"],
                     writes=["psfence"], tag=f"aa_Tt{b}")
        def evl(e, pi=pendT):
            src = c.pt[pi % 2][:].rearrange("p (j t) -> p j t", j=8)
            dstap = c.XLNF[:].rearrange("p (j s) -> p j s", j=8)[:, :, 128*pi:128*(pi+1)]
            return e.tensor_copy(dstap, src)
        P.op("vector", evl, reads=[f"pt{pendT%2}"], writes=["outfm"], after=[tailT],
             tag=f"aa_Tev{b}_{pendT}")
        P.barrier()
        emit_load_w1024(P, c, W["Wo"], tag=f"aa_ow{b}")
        pend = None
        for tt in range(NT):
            banks = ((2*tt) % 4, (2*tt + 1) % 4)
            def mmd(e, tt=tt, banks=banks):
                for h in range(2):
                    for r in range(8):
                        e.matmul(c.ps[banks[h]][:],
                                 c.XLNF[:, 2048*r + 128*tt:2048*r + 128*(tt+1)],
                                 c.WB[:, 1024*r + 512*h:1024*r + 512*h + 512],
                                 start=(r == 0), stop=(r == 7))
                return pe_fence(e, c)
            mmop = P.op("tensor", mmd, reads=["outfm", "wcur", "ident"],
                        writes=[f"ps{banks[0]}", f"ps{banks[1]}", "psfence"], tag=f"aa_d{b}_{tt}")
            if pend is not None:
                pi, pbk = pend
                emit_resid_store(P, c, xb, ob, pi, (c.ps[pbk[0]][:], c.ps[pbk[1]][:]),
                                 (f"ps{pbk[0]}", f"ps{pbk[1]}"), after=[mmop], tag=f"ar{b}_")
            pend = (tt, banks)
        tailD = P.op("tensor", lambda e: tail_fence(e, c), reads=["ident"],
                     writes=["psfence"], tag=f"aa_dt{b}")
        pi, pbk = pend
        emit_resid_store(P, c, xb, ob, pi, (c.ps[pbk[0]][:], c.ps[pbk[1]][:]),
                         (f"ps{pbk[0]}", f"ps{pbk[1]}"), after=[tailD], tag=f"ar{b}_")

def emit_kvg_evac(P, c, b, name, pend, after_op):
    i, banks = pend
    for h in range(2):
        dst = {"k": c.B1, "v": c.B2, "g": c.B3}[name]
        dap = dst[:, 1024*i + 512*h:1024*i + 512*(h+1)]
        key = {"k": "ktm", "v": "vtm", "g": "gout"}[name]
        if name == "k":
            emit_elu1_evac(P, c, c.ps[banks[h]][:], f"ps{banks[h]}", dap, key, after_op,
                           tag=f"aa_ke{b}_{i}_{h}_")
        elif name == "v":
            P.op("vector", lambda e, dap=dap, h=h: e.tensor_copy(dap, c.ps[banks[h]][:]),
                 reads=[f"ps{banks[h]}"], writes=[key], after=[after_op], tag=f"aa_ve{b}_{i}_{h}")
        else:
            P.op("scalar", lambda e, dap=dap, h=h: e.activation(dap, c.ps[banks[h]][:], AF.Sigmoid),
                 reads=[f"ps{banks[h]}"], writes=[key], after=[after_op], tag=f"aa_ge{b}_{i}_{h}")

def emit_gla_out_evac(P, c, b, hh, pend, after_op, SCALE):
    i, bank = pend
    zrec = c.ST[:, 50:51]
    P.op("vector", lambda e, bank=bank: e.tensor_scalar(zrec, c.ps[bank][:, 64:65], 1e-6,
                                                        None, ALU.max),
         reads=[f"ps{bank}"], writes=["zrec"], after=[after_op], tag=f"aa_z1{b}_{hh}_{i}")
    P.op("vector", lambda e: e.reciprocal(zrec, zrec),
         reads=["zrec"], writes=["zrec"], tag=f"aa_z2{b}_{hh}_{i}")
    P.op("vector", lambda e: e.tensor_scalar(zrec, zrec, SCALE, None, ALU.mult),
         reads=["zrec"], writes=["zrec"], tag=f"aa_z3{b}_{hh}_{i}")
    tmp = c.FX[:, 3584:3648]
    P.op("vector", lambda e, bank=bank: e.tensor_scalar(tmp, c.ps[bank][:, 0:64], zrec,
                                                        None, ALU.mult),
         reads=[f"ps{bank}", "zrec"], writes=["gtmp"], tag=f"aa_z4{b}_{hh}_{i}")
    gap = c.B3[:, 1024*i + 64*hh:1024*i + 64*(hh+1)]
    P.op("vector", lambda e, gap=gap: e.tensor_tensor(gap, tmp, gap, ALU.mult),
         reads=["gtmp", "gout"], writes=["gout"], tag=f"aa_z5{b}_{hh}_{i}")

def emit_load_w1024_cols(P, c, w_dram, col0, ncols, tag):
    for r in range(8):
        stage = c.FW[:, 1024 + 128*(r % 2):1024 + 128*(r % 2) + ncols]
        P.op("sync", lambda e, r=r, stage=stage: e.dma_start(
                stage, w_dram[128*r:128*(r+1), col0:col0 + ncols]),
             writes=[f"qwstage{r%2}"], dma=True, tag=f"{tag}ld{r}")
        P.op("vector", lambda e, r=r, stage=stage: e.tensor_copy(
                c.WB[:, 2176 + ncols*r:2176 + ncols*(r+1)], stage),
             reads=[f"qwstage{r%2}"], writes=["qwcur"], tag=f"{tag}c{r}")

def emit_L0_attn_b(P, c, x_dram, out_dram, W):
    for b in range(B):
        P.barrier()
        xb = x_dram[S*b:S*(b+1), :]
        ob = out_dram[S*b:S*(b+1), :]
        emit_ln_b(P, c, xb, tag=f"S{b}")
        emit_load_w1024(P, c, W["Wv2"], tag=f"sb_vw{b}")
        pend = None
        for i in range(NT):
            banks = ((2*i) % 4, (2*i + 1) % 4)
            def mmv(e, i=i, banks=banks):
                for h in range(2):
                    for r in range(8):
                        e.matmul(c.ps[banks[h]][:],
                                 c.XLNF[:, 2048*r + 128*i:2048*r + 128*(i+1)],
                                 c.WB[:, 1024*r + 512*h:1024*r + 512*(h+1)],
                                 start=(r == 0), stop=(r == 7))
                return pe_fence(e, c)
            mmop = P.op("tensor", mmv, reads=["xlnfm", "wcur", "ident"],
                        writes=[f"ps{banks[0]}", f"ps{banks[1]}", "psfence"], tag=f"sb_v{b}_{i}")
            if pend is not None:
                pi, pbk = pend
                for h in range(2):
                    P.op("vector", lambda e, pi=pi, h=h, pbk=pbk: e.tensor_copy(
                            c.B2[:, 1024*pi + 512*h:1024*pi + 512*(h+1)], c.ps[pbk[h]][:]),
                         reads=[f"ps{pbk[h]}"], writes=["vtm"], after=[mmop],
                         tag=f"sb_ve{b}_{pi}_{h}")
            pend = (i, banks)
        tail = P.op("tensor", lambda e: tail_fence(e, c), reads=["ident"],
                    writes=["psfence"], tag=f"sb_vt{b}")
        pi, pbk = pend
        for h in range(2):
            P.op("vector", lambda e, pi=pi, h=h, pbk=pbk: e.tensor_copy(
                    c.B2[:, 1024*pi + 512*h:1024*pi + 512*(h+1)], c.ps[pbk[h]][:]),
                 reads=[f"ps{pbk[h]}"], writes=["vtm"], after=[tail], tag=f"sb_ve{b}_{pi}_{h}")
        P.barrier()
        for hh in range(16):
            hp = hh % 2
            qfm = c.WB[0:64, 4096*hp:4096*hp + 2048]
            kfm = c.WB[0:64, 4096*hp + 2048:4096*hp + 4096]
            qk, kk = f"qfm{hp}", f"kfm{hp}"
            woq = 8192 + 1024*hp
            wok = woq + 512
            for wname, off, wkey in (("Wq", woq, f"qw{hp}"), ("Wk", wok, f"kw{hp}")):
                for r in range(8):
                    stage = c.FW[:, 1024 + 128*(r % 2):1024 + 128*(r % 2) + 64]
                    P.op("sync", lambda e, wname=wname, r=r, stage=stage, hh=hh: e.dma_start(
                            stage, W[wname][hh, 128*r:128*(r+1), :]),
                         writes=[f"qwstage{r%2}"], dma=True, tag=f"sb_{wkey}{b}_{hh}_{r}")
                    P.op("vector", lambda e, r=r, off=off, stage=stage: e.tensor_copy(
                            c.WB[:, off + 64*r:off + 64*(r+1)], stage),
                         reads=[f"qwstage{r%2}"], writes=[wkey], tag=f"sb_{wkey}c{b}_{hh}_{r}")
            pend = None
            for kq in range(8):
                which, n4 = divmod(kq, 4)
                bank = kq % 4
                woff = woq if which == 0 else wok
                wk = f"qw{hp}" if which == 0 else f"kw{hp}"
                def mmq(e, n4=n4, bank=bank, woff=woff):
                    for r in range(8):
                        e.matmul(c.ps[bank][0:64, :], c.WB[:, woff + 64*r:woff + 64*(r+1)],
                                 c.XLNF[:, 2048*r + 512*n4:2048*r + 512*(n4+1)],
                                 start=(r == 0), stop=(r == 7))
                    return pe_fence(e, c)
                mmop = P.op("tensor", mmq, reads=[wk, "xlnfm", "ident"],
                            writes=[f"ps{bank}", "psfence"], tag=f"sb_q{b}_{hh}_{kq}")
                if pend is not None:
                    pw, pn4, pb = pend
                    dst = qfm if pw == 0 else kfm
                    dk = qk if pw == 0 else kk
                    P.op("vector", lambda e, pn4=pn4, pb=pb, dst=dst: e.tensor_copy(
                            dst[:, 512*pn4:512*(pn4+1)], c.ps[pb][0:64, :]),
                         reads=[f"ps{pb}"], writes=[dk], after=[mmop],
                         tag=f"sb_qe{b}_{hh}_{pw}_{pn4}")
                pend = (which, n4, bank)
            tailq = P.op("tensor", lambda e: tail_fence(e, c), reads=["ident"],
                         writes=["psfence"], tag=f"sb_qt{b}_{hh}")
            pw, pn4, pb = pend
            P.op("vector", lambda e, pn4=pn4, pb=pb, kfm=kfm: e.tensor_copy(
                    kfm[:, 512*pn4:512*(pn4+1)], c.ps[pb][0:64, :]),
                 reads=[f"ps{pb}"], writes=[kk], after=[tailq], tag=f"sb_qe{b}_{hh}_1_{pn4}")
            P.settle(c, [qk, kk], tag=f"sb_settle{b}_{hh}")
            if SB_DEBUG is not None and b == 3:
                P.op("sync", lambda e, hh=hh, qfm=qfm: e.dma_start(
                        SB_DEBUG["qk"][64*hh:64*(hh+1), 0:2048], qfm),
                     reads=[qk], dma=True, tag=f"sbdbg_q{hh}")
                P.op("sync", lambda e, hh=hh, kfm=kfm: e.dma_start(
                        SB_DEBUG["qk"][64*hh:64*(hh+1), 2048:4096], kfm),
                     reads=[kk], dma=True, tag=f"sbdbg_k{hh}")
            tpend = None
            apend = None
            oepend = [None]
            def emit_oe(pi, after_op):
                P.op("scalar", lambda e, pi=pi, hh=hh: e.activation(
                        c.B1[:, 1024*pi + 64*hh:1024*pi + 64*(hh+1)],
                        c.ps[5][:, 64*(pi % 4):64*(pi % 4) + 64],
                        AF.Copy, scale=c.ST[:, 60 + pi % 4:61 + pi % 4]),
                     reads=[f"av{pi%4}", f"zin{pi%4}"], writes=["outat"], after=[after_op],
                     tag=f"sb_oe{b}_{hh}_{pi}")
            def emit_trn_te(j, hh=hh, b=b):
                att_j = c.B3[:, 2048*(j % 2):2048*(j % 2 + 1)]
                def trn(e, att=att_j):
                    for t in range(16):
                        pb_ = c.pt[t // 8]
                        e.transpose(pb_[:, 128*(t % 8):128*(t % 8 + 1)],
                                    att[:, 128*t:128*(t+1)], c.ident)
                    return pe_fence(e, c)
                top = P.op("tensor", trn, reads=[f"att{j%2}", "ident"],
                           writes=["pt0", "pt1", "psfence"], tag=f"sb_T{b}_{hh}_{j}")
                atT = c.B3[:, 4096 + 2048*(j % 2):4096 + 2048*(j % 2 + 1)]
                for half in range(2):
                    P.op("vector", lambda e, half=half, atT=atT: e.tensor_copy(
                            atT[:, 1024*half:1024*(half+1)], c.pt[half][:]),
                         reads=[f"pt{half}"], writes=[f"atT{j%2}"], after=[top],
                         tag=f"sb_Te{b}_{hh}_{j}_{half}")
                return top
            def emit_av(j, hh=hh, b=b):
                atT = c.B3[:, 4096 + 2048*(j % 2):4096 + 2048*(j % 2 + 1)]
                def av(e, atT=atT, hh=hh, j=j):
                    sl = slice(64*(j % 4), 64*(j % 4) + 64)
                    for t in range(NT):
                        e.matmul(c.ps[5][:, sl], atT[:, 128*t:128*(t+1)],
                                 c.B2[:, 1024*t + 64*hh:1024*t + 64*(hh+1)],
                                 start=(t == 0), stop=(t == NT - 1))
                    return pe_fence(e, c)
                avop = P.op("tensor", av, reads=[f"atT{j%2}", "vtm", "ident"],
                            writes=[f"av{j%4}", "psfence"], tag=f"sb_av{b}_{hh}_{j}")
                if oepend[0] is not None:
                    emit_oe(oepend[0], avop)
                oepend[0] = j
            for i in range(NT):
                def smm(e, i=i, qfm=qfm, kfm=kfm):
                    for tq in range(4):
                        e.matmul(c.ps[tq][:], qfm[:, 128*i:128*(i+1)],
                                 kfm[:, 512*tq:512*(tq+1)], start=True, stop=True)
                    pe_fence(e, c)
                    return tail_fence(e, c, n=6)
                sop = P.op("tensor", smm, reads=[qk, kk, "ident"],
                           writes=["ps0", "ps1", "ps2", "ps3", "psfence"],
                           tag=f"sb_s{b}_{hh}_{i}")
                if tpend is not None:
                    emit_trn_te(tpend)
                araw = c.FX[:, 0:2048] if i % 2 == 0 else c.FW[:, 2048:4096]
                akey = f"araw{i%2}"
                for tq in range(4):
                    P.op("scalar", lambda e, tq=tq, araw=araw: e.activation(
                            araw[:, 512*tq:512*(tq+1)], c.ps[tq][:], AF.Copy),
                         reads=[f"ps{tq}"], writes=[akey], after=[sop],
                         tag=f"sb_se{b}_{hh}_{i}_{tq}")
                nm = c.ST[:, 56 + i % 2:57 + i % 2]
                nmk = f"nm{i%2}"
                P.op("vector", lambda e, araw=araw, nm=nm: e.tensor_reduce(
                        nm, araw, AX, ALU.max, negate=True),
                     reads=[akey], writes=[nmk], tag=f"sb_mx{b}_{hh}_{i}")
                att = c.B3[:, 2048*(i % 2):2048*(i % 2 + 1)]
                zc = c.ST[:, 58 + i % 2:59 + i % 2]
                zck = f"zc{i%2}"
                P.op("scalar", lambda e, araw=araw, att=att, nm=nm, zc=zc: e.activation(
                        att, araw, AF.Exp, bias=nm, accum_out=zc),
                     reads=[akey, nmk], writes=[f"att{i%2}", zck], tag=f"sb_ex{b}_{hh}_{i}")
                if SB_DEBUG is not None and b == 3 and i == 5:
                    P.op("sync", lambda e, hh=hh, araw=araw: e.dma_start(
                            SB_DEBUG["sc"][128*hh:128*(hh+1), :], araw),
                         reads=[akey], dma=True, tag=f"sbdbg_s{hh}")
                    P.op("sync", lambda e, hh=hh, att=att: e.dma_start(
                            SB_DEBUG["at"][128*hh:128*(hh+1), :], att),
                         reads=[f"att{i%2}"], dma=True, tag=f"sbdbg_a{hh}")
                zin = c.ST[:, 60 + i % 4:61 + i % 4]
                zik = f"zin{i%4}"
                P.op("vector", lambda e, zc=zc, zin=zin: e.reciprocal(zin, zc),
                     reads=[zck], writes=[zik], tag=f"sb_z2{b}_{hh}_{i}")
                P.op("vector", lambda e, zin=zin: e.tensor_scalar(zin, zin, 0.125, None, ALU.mult),
                     reads=[zik], writes=[zik], tag=f"sb_z3{b}_{hh}_{i}")
                if apend is not None:
                    emit_av(apend)
                apend = tpend
                tpend = i
            emit_trn_te(tpend)
            emit_av(apend)
            emit_av(tpend)
            avtail = P.op("tensor", lambda e: tail_fence(e, c), reads=["ident"],
                          writes=["psfence"], tag=f"sb_avt{b}_{hh}")
            emit_oe(oepend[0], avtail)
        P.barrier()
        for tt in range(NT):
            xt = c.FX[:, 2048 + 1024*(tt % 2):2048 + 1024*(tt % 2 + 1)]
            key = f"sr{b}_rx{tt%2}"
            P.op("sync", lambda e, xt=xt, tt=tt, xb=xb: e.dma_start(xt, xb[128*tt:128*(tt+1), :]),
                 writes=[key], dma=True, tag=f"sr{b}_ld{tt}")
            P.op("vector", lambda e, xt=xt, tt=tt: e.tensor_add(
                    xt, xt, c.B1[:, 1024*tt:1024*(tt+1)]),
                 reads=[key, "outat"], writes=[key], tag=f"sr{b}_add{tt}")
            P.op("sync", lambda e, xt=xt, tt=tt, ob=ob: e.dma_start(ob[128*tt:128*(tt+1), :], xt),
                 reads=[key], dma=True, tag=f"sr{b}_st{tt}")

def host_consts():
    identf = np.eye(128, dtype=np.float32)
    ident = np.eye(128, dtype=ml_dtypes.bfloat16)
    i2 = np.concatenate([np.eye(64), np.eye(64)], axis=0).astype(ml_dtypes.bfloat16)
    sel = np.zeros((16, 1024), dtype=ml_dtypes.bfloat16)
    for cc in range(8):
        for row in range(128):
            bb = 2*cc + (1 if row >= 64 else 0)
            sel[bb, 128*cc + row] = 1.0
    return {"ident": ident, "i2": i2, "sel": sel, "identf": identf}

def host_dft_consts():
    N = 4096
    l = np.arange(2048)
    f = np.arange(2176)
    ang = 2 * np.pi * np.outer(l, f) / N
    Fc = np.cos(ang); Fs = -np.sin(ang)
    Fc[:, 2049:] = 0; Fs[:, 2049:] = 0
    ang2 = 2 * np.pi * np.outer(l + 2048, f) / N
    Fc2 = np.cos(ang2); Fs2 = -np.sin(ang2)
    Fc2[:, 2049:] = 0; Fs2[:, 2049:] = 0
    fv = np.arange(2176).astype(np.float64)
    w = np.where(fv == 0, 1.0, 2.0); w[2048] = 1.0; w[2049:] = 0.0
    Gc = (w[:, None] * np.cos(2 * np.pi * np.outer(fv, l) / N)) / N
    Gs = (-w[:, None] * np.sin(2 * np.pi * np.outer(fv, l) / N)) / N
    bf = lambda a: np.ascontiguousarray(a).astype(ml_dtypes.bfloat16)
    return {"Fc": bf(Fc), "Fs": bf(Fs), "Fc2": bf(Fc2), "Fs2": bf(Fs2),
            "Gc": bf(Gc), "Gs": bf(Gs)}

def _ac_kf_evac(P, c, pend, after_op, kfr, kfi):
    kh, fp, ri, ct, bank = pend
    slot = 512 * bank
    skey = f"ac_kev{bank}"
    if ri == 0:
        def ev(e, slot=slot, bank=bank, kh=kh, ct=ct):
            return e.tensor_tensor(c.B3[:, slot:slot+512], c.ps[bank][:],
                                   c.FX[:, 1024*kh + 512*ct:1024*kh + 512*(ct+1)],
                                   ALU.add)
        reads = [f"ps{bank}", "ac_dbc"]
        P.op("vector", ev, reads=reads, writes=[skey], after=[after_op],
             tag=f"ac_kev{kh}_{fp}_{ri}_{ct}")
    else:
        def ev(e, slot=slot, bank=bank):
            return e.activation(c.B3[:, slot:slot+512], c.ps[bank][:], AF.Copy)
        P.op("scalar", ev, reads=[f"ps{bank}"], writes=[skey], after=[after_op],
             tag=f"ac_kev{kh}_{fp}_{ri}_{ct}")
    dst = kfr if ri == 0 else kfi
    P.op("gpsimd", lambda e, dst=dst, fp=fp, kh=kh, ct=ct, slot=slot: e.dma_start(
            dst[128*fp:128*(fp+1), 1024*kh + 512*ct:1024*kh + 512*(ct+1)],
            c.B3[:, slot:slot+512]),
         reads=[skey], writes=["ac_kfd"], dma=True, tag=f"ac_kst{kh}_{fp}_{ri}_{ct}")

def _ac_z_evac(P, c, b, pend, after_op, zscr):
    ct, lb, bank = pend
    slot = 512 * bank
    skey = f"ac_zs{bank}"
    P.op("scalar", lambda e, slot=slot, bank=bank: e.activation(
            c.B3[:, slot:slot+512], c.ps[bank][:], AF.Silu),
         reads=[f"ps{bank}"], writes=[skey], after=[after_op], tag=f"ac_zev{b}_{ct}_{lb}")
    P.op("gpsimd", lambda e, lb=lb, ct=ct, slot=slot: e.dma_start(
            zscr[128*lb:128*(lb+1), 512*ct:512*(ct+1)], c.B3[:, slot:slot+512]),
         reads=[skey], writes=["ac_zscrd"], dma=True, tag=f"ac_zst{b}_{ct}_{lb}")

def emit_L1_attn_a(P, c, x_dram, out_dram, W, scratch, prep=True):
    winb, woutb = scratch["winb"], scratch["woutb"]
    zscr, ysr, ysi = scratch["zscr"], scratch["ysr"], scratch["ysi"]
    kfr, kfi = scratch["kfr"], scratch["kfi"]
    rot = [0]
    def nb():
        b_ = rot[0]; rot[0] = (rot[0] + 1) % 4
        return b_
    FWB = c.FW[:].bitcast(BF16)
    if prep:
        for name, src, dst, nrow, cw, wkey in (
                ("wi", W["winT"], winb, 1024, 2048, "ac_winb"),
                ("wo", W["woutT"], woutb, 2048, 1024, "ac_woutb")):
            nh = 4096 // cw if name == "wi" else 1
            for r in range(nrow // 128):
                for hh in range(nh):
                    par = (r * nh + hh) % 2
                    P.op("sync", lambda e, src=src, r=r, hh=hh, par=par, cw=cw: e.dma_start(
                            c.FW[:, 2048*par:2048*par + cw],
                            src[128*r:128*(r+1), cw*hh:cw*(hh+1)]),
                         writes=[f"ac_cst{par}"], dma=True, tag=f"ac_{name}ld{r}_{hh}")
                    P.op("vector", lambda e, par=par, cw=cw: e.tensor_copy(
                            c.B3[:, 2048*par:2048*par + cw], c.FW[:, 2048*par:2048*par + cw]),
                         reads=[f"ac_cst{par}"], writes=[f"ac_csb{par}"], tag=f"ac_{name}c{r}_{hh}")
                    P.op("gpsimd", lambda e, dst=dst, r=r, hh=hh, par=par, cw=cw: e.dma_start(
                            dst[128*r:128*(r+1), cw*hh:cw*(hh+1)],
                            c.B3[:, 2048*par:2048*par + cw]),
                         reads=[f"ac_csb{par}"], writes=[wkey], dma=True, tag=f"ac_{name}st{r}_{hh}")
        P.barrier()
        P.op("sync", lambda e: e.dma_start(c.ST[:, 96:160], W["cwr"]),
             writes=["ac_cwr"], dma=True, tag="ac_ldcwr")
        P.op("sync", lambda e: e.dma_start(c.ST[:, 160:176], W["cbr"]),
             writes=["ac_cbr"], dma=True, tag="ac_ldcbr")
        P.op("sync", lambda e: e.dma_start(c.FX[:, 0:2048], W["dsk"].broadcast_to([128, 2048])),
             writes=["ac_dbc"], dma=True, tag="ac_dbc")
        for kh in range(2):
            P.barrier()
            for lc in range(32):
                par = lc % 2
                arena = c.B1 if lc < 16 else c.B2
                P.op("sync", lambda e, lc=lc, kh=kh, par=par: e.dma_start(
                        c.FW[:, 2048 + 1024*par:2048 + 1024*(par+1)],
                        W["kT"][128*lc:128*(lc+1), 1024*kh:1024*(kh+1)]),
                     writes=[f"ac_kcs{par}"], dma=True, tag=f"ac_kld{kh}_{lc}")
                P.op("vector", lambda e, lc=lc, par=par, arena=arena: e.tensor_copy(
                        arena[:, 1024*(lc % 16):1024*(lc % 16 + 1)],
                        c.FW[:, 2048 + 1024*par:2048 + 1024*(par+1)]),
                     reads=[f"ac_kcs{par}"], writes=["ac_ktres"], tag=f"ac_kc{kh}_{lc}")
            P.barrier()
            pend = None
            for fp in range(17):
                fpp = fp % 2
                farena = c.WB if fpp == 0 else c.B3
                fbase = 0 if fpp == 0 else 2048
                for mi, mname in enumerate(("Fc", "Fc2", "Fs", "Fs2")):
                    P.op("sync", lambda e, mi=mi, mname=mname, fp=fp, farena=farena, fbase=fbase: e.dma_start(
                            farena[:, fbase+2048*mi:fbase+2048*(mi+1)].rearrange("p (l r) -> p l r", l=16),
                            W[mname].rearrange("(l p) f -> p l f", p=128)[:, :, 128*fp:128*(fp+1)]),
                         writes=[f"ac_kfs{fpp}_{mi}"], dma=True, tag=f"ac_kF{kh}_{fp}_{mi}")
                for ri in range(2):
                    for ct in range(2):
                        bank = nb()
                        def mmk(e, ri=ri, ct=ct, bank=bank, farena=farena, fbase=fbase):
                            for lc in range(32):
                                arena = c.B1 if lc < 16 else c.B2
                                wcol = fbase + 2048*(2*ri + (0 if lc < 16 else 1)) + 128*(lc % 16)
                                e.matmul(c.ps[bank][:], farena[:, wcol:wcol+128],
                                         arena[:, 1024*(lc % 16) + 512*ct:1024*(lc % 16) + 512*(ct+1)],
                                         start=(lc == 0), stop=(lc == 31))
                            return pe_fence(e, c)
                        mmop = P.op("tensor", mmk,
                                    reads=[f"ac_kfs{fpp}_0", f"ac_kfs{fpp}_1", f"ac_kfs{fpp}_2",
                                           f"ac_kfs{fpp}_3", "ac_ktres", "ident"],
                                    writes=[f"ps{bank}", "psfence"], tag=f"ac_kmm{kh}_{fp}_{ri}_{ct}")
                        if pend is not None:
                            _ac_kf_evac(P, c, pend, mmop, kfr, kfi)
                        pend = (kh, fp, ri, ct, bank)
            tail = P.op("tensor", lambda e: tail_fence(e, c), reads=["ident"],
                        writes=["psfence"], tag=f"ac_ktail{kh}")
            _ac_kf_evac(P, c, pend, tail, kfr, kfi)
        P.barrier()
    for b in range(B):
        xb = x_dram[S*b:S*(b+1), :]
        ob = out_dram[S*b:S*(b+1), :]
        P.barrier()
        emit_ln_b(P, c, xb, tag=f"ac{b}")
        P.barrier()
        pend = None
        for ct in range(4):
            wslot = 2048 + 4096*(ct % 2)
            P.op("sync", lambda e, ct=ct, wslot=wslot: e.dma_start(
                    c.WB[:, wslot:wslot+4096].rearrange("p (j r) -> p j r", j=8),
                    winb.rearrange("(j p) q -> p j q", p=128)[:, :, 2048+512*ct:2048+512*(ct+1)]),
                 reads=["ac_winb"], writes=[f"ac_wz{ct%2}"], dma=True, tag=f"ac_z{b}_ldw{ct}")
            for lb in range(16):
                bank = nb()
                def mmz(e, lb=lb, bank=bank, wslot=wslot):
                    for j in range(8):
                        e.matmul(c.ps[bank][:], c.XLNF[:, 2048*j+128*lb:2048*j+128*(lb+1)],
                                 c.WB[:, wslot+512*j:wslot+512*(j+1)],
                                 start=(j == 0), stop=(j == 7))
                    return pe_fence(e, c)
                mmop = P.op("tensor", mmz, reads=[f"ac_wz{ct%2}", "xlnfm", "ident"],
                            writes=[f"ps{bank}", "psfence"], tag=f"ac_zmm{b}_{ct}_{lb}")
                if pend is not None:
                    _ac_z_evac(P, c, b, pend, mmop, zscr)
                pend = (ct, lb, bank)
        tail = P.op("tensor", lambda e: tail_fence(e, c), reads=["ident"],
                    writes=["psfence"], tag=f"ac_ztail{b}")
        _ac_z_evac(P, c, b, pend, tail, zscr)
        P.barrier()
        def flush_conv(cc_, b=b):
            par = cc_ % 2
            xpre = c.FX[:, 2048*par:2048*(par+1)]
            acc = c.FW[:, 0:2048]
            xkeys = [f"ac_xpre{par}_{t}" for t in range(4)]
            P.op("vector", lambda e, cc_=cc_, xpre=xpre, acc=acc: e.tensor_scalar(
                    acc, xpre, c.ST[:, 96+4*cc_+3:96+4*cc_+4], c.ST[:, 160+cc_:161+cc_],
                    ALU.mult, ALU.add),
                 reads=xkeys + ["ac_cwr", "ac_cbr"], writes=["ac_acc"], tag=f"ac_cv{b}_{cc_}_0")
            for jj in (1, 2, 3):
                P.op("vector", lambda e, cc_=cc_, jj=jj, xpre=xpre, acc=acc: e.scalar_tensor_tensor(
                        acc[:, jj:2048], xpre[:, 0:2048-jj], c.ST[:, 96+4*cc_+3-jj:96+4*cc_+4-jj],
                        acc[:, jj:2048], ALU.mult, ALU.add),
                     reads=xkeys + ["ac_cwr", "ac_acc"], writes=["ac_acc"], tag=f"ac_cv{b}_{cc_}_{jj}")
            P.op("scalar", lambda e, acc=acc: e.activation(c.B3[:, 0:2048], acc, AF.Silu),
                 reads=["ac_acc"], writes=["ac_xcv"], tag=f"ac_si{b}_{cc_}")
            def tg(e):
                for lc in range(16):
                    dstb = c.pt[0] if lc < 8 else c.pt[1]
                    e.transpose(dstb[:, 128*(lc % 8):128*(lc % 8 + 1)],
                                c.B3[:, 128*lc:128*(lc+1)], c.ident)
                return pe_fence(e, c)
            tgop = P.op("tensor", tg, reads=["ac_xcv", "ident"],
                        writes=["pt0", "pt1", "psfence"], tag=f"ac_tg{b}_{cc_}")
            for half in range(2):
                arena = c.B1 if half == 0 else c.B2
                P.op("vector", lambda e, cc_=cc_, half=half, arena=arena: e.tensor_copy(
                        arena[:].rearrange("p (l r) -> p l r", l=8)[:, :, 128*cc_:128*(cc_+1)],
                        c.pt[half][:].rearrange("p (l r) -> p l r", l=8)),
                     reads=[f"pt{half}"], writes=["ac_xcvT"], after=[tgop],
                     tag=f"ac_tev{b}_{cc_}_{half}")
        evpend = None
        for cc in range(16):
            wslot = 1024*(cc % 2)
            P.op("sync", lambda e, cc=cc, wslot=wslot: e.dma_start(
                    c.WB[:, wslot:wslot+1024].rearrange("p (j r) -> p j r", j=8),
                    winb.rearrange("(j p) q -> p j q", p=128)[:, :, 128*cc:128*(cc+1)]),
                 reads=["ac_winb"], writes=[f"ac_wa{cc%2}"], dma=True, tag=f"ac_w{b}_ldw{cc}")
            for lt in range(4):
                bank = nb()
                def mmx(e, lt=lt, bank=bank, wslot=wslot):
                    for j in range(8):
                        e.matmul(c.ps[bank][:], c.WB[:, wslot+128*j:wslot+128*(j+1)],
                                 c.XLNF[:, 2048*j+512*lt:2048*j+512*(lt+1)],
                                 start=(j == 0), stop=(j == 7))
                    return pe_fence(e, c)
                mmop = P.op("tensor", mmx, reads=[f"ac_wa{cc%2}", "xlnfm", "ident"],
                            writes=[f"ps{bank}", "psfence"], tag=f"ac_xmm{b}_{cc}_{lt}")
                if evpend is not None:
                    pcc, plt, pbank = evpend
                    P.op("scalar", lambda e, pcc=pcc, plt=plt, pbank=pbank: e.activation(
                            c.FX[:, 2048*(pcc % 2) + 512*plt:2048*(pcc % 2) + 512*(plt+1)],
                            c.ps[pbank][:], AF.Copy),
                         reads=[f"ps{pbank}"], writes=[f"ac_xpre{pcc%2}_{plt}"], after=[mmop],
                         tag=f"ac_xev{b}_{pcc}_{plt}")
                    if plt == 3:
                        flush_conv(pcc)
                evpend = (cc, lt, bank)
        tailw = P.op("tensor", lambda e: tail_fence(e, c), reads=["ident"],
                     writes=["psfence"], tag=f"ac_wtail{b}")
        pcc, plt, pbank = evpend
        P.op("scalar", lambda e, pcc=pcc, plt=plt, pbank=pbank: e.activation(
                c.FX[:, 2048*(pcc % 2) + 512*plt:2048*(pcc % 2) + 512*(plt+1)],
                c.ps[pbank][:], AF.Copy),
             reads=[f"ps{pbank}"], writes=[f"ac_xpre{pcc%2}_{plt}"], after=[tailw],
             tag=f"ac_xev{b}_{pcc}_{plt}")
        flush_conv(15)
        P.barrier()
        def pw_spill(fp, b=b):
            par = fp % 2
            kbase = 0 if par == 0 else 8192
            rk = [f"ac_xre{t}" for t in range(4)]
            ik = [f"ac_xim{t}" for t in range(4)]
            P.op("vector", lambda e, kbase=kbase: e.tensor_tensor(
                    c.FW[:, 0:2048], c.FX[:, 0:2048], c.B3[:, kbase:kbase+2048], ALU.mult),
                 reads=rk + [f"ac_kfr{par}"], writes=["ac_pw1"], tag=f"ac_pa{b}_{fp}")
            P.op("vector", lambda e, kbase=kbase: e.tensor_tensor(
                    c.FW[:, 2048:4096], c.FX[:, 2048:4096], c.B3[:, kbase+2048:kbase+4096], ALU.mult),
                 reads=ik + [f"ac_kfi{par}"], writes=["ac_pw2"], tag=f"ac_pb{b}_{fp}")
            P.op("vector", lambda e: e.tensor_tensor(
                    c.B3[:, 4096:6144], c.FW[:, 0:2048], c.FW[:, 2048:4096], ALU.subtract),
                 reads=["ac_pw1", "ac_pw2"], writes=["ac_yrs"], tag=f"ac_pc{b}_{fp}")
            P.op("vector", lambda e, kbase=kbase: e.tensor_tensor(
                    c.FW[:, 0:2048], c.FX[:, 0:2048], c.B3[:, kbase+2048:kbase+4096], ALU.mult),
                 reads=rk + [f"ac_kfi{par}"], writes=["ac_pw1"], tag=f"ac_pd{b}_{fp}")
            P.op("vector", lambda e, kbase=kbase: e.tensor_tensor(
                    c.FW[:, 2048:4096], c.FX[:, 2048:4096], c.B3[:, kbase:kbase+2048], ALU.mult),
                 reads=ik + [f"ac_kfr{par}"], writes=["ac_pw2"], tag=f"ac_pe{b}_{fp}")
            P.op("vector", lambda e: e.tensor_tensor(
                    c.B3[:, 6144:8192], c.FW[:, 0:2048], c.FW[:, 2048:4096], ALU.add),
                 reads=["ac_pw1", "ac_pw2"], writes=["ac_yis"], tag=f"ac_pf{b}_{fp}")
            P.op("gpsimd", lambda e, fp=fp: e.dma_start(
                    ysr[128*fp:128*(fp+1), :], c.B3[:, 4096:6144]),
                 reads=["ac_yrs"], writes=["ac_ysd"], dma=True, tag=f"ac_sr{b}_{fp}")
            P.op("gpsimd", lambda e, fp=fp: e.dma_start(
                    ysi[128*fp:128*(fp+1), :], c.B3[:, 6144:8192]),
                 reads=["ac_yis"], writes=["ac_ysd"], dma=True, tag=f"ac_si2{b}_{fp}")
        def fwd_evac(pend_, after_op, b=b):
            fp_, ri, ct, bank = pend_
            key = f"ac_xre{ct}" if ri == 0 else f"ac_xim{ct}"
            P.op("scalar", lambda e, ri=ri, ct=ct, bank=bank: e.activation(
                    c.FX[:, 2048*ri + 512*ct:2048*ri + 512*(ct+1)], c.ps[bank][:], AF.Copy),
                 reads=[f"ps{bank}"], writes=[key], after=[after_op],
                 tag=f"ac_fev{b}_{fp_}_{ri}_{ct}")
        fpend = None
        for fp in range(17):
            par = fp % 2
            fslot = 4352*par
            for mi, mname in ((0, "Fc"), (1, "Fs")):
                P.op("sync", lambda e, mi=mi, mname=mname, fp=fp, fslot=fslot: e.dma_start(
                        c.WB[:, fslot+2048*mi:fslot+2048*(mi+1)].rearrange("p (l r) -> p l r", l=16),
                        W[mname].rearrange("(l p) f -> p l f", p=128)[:, :, 128*fp:128*(fp+1)]),
                     writes=[f"ac_fst{par}"], dma=True, tag=f"ac_fF{b}_{fp}_{mi}")
            kbase = 0 if par == 0 else 8192
            P.op("sync", lambda e, fp=fp, kbase=kbase: e.dma_start(
                    c.B3[:, kbase:kbase+2048], kfr[128*fp:128*(fp+1), :]),
                 reads=["ac_kfd"], writes=[f"ac_kfr{par}"], dma=True, tag=f"ac_fkr{b}_{fp}")
            P.op("sync", lambda e, fp=fp, kbase=kbase: e.dma_start(
                    c.B3[:, kbase+2048:kbase+4096], kfi[128*fp:128*(fp+1), :]),
                 reads=["ac_kfd"], writes=[f"ac_kfi{par}"], dma=True, tag=f"ac_fki{b}_{fp}")
            for ri in range(2):
                for ct in range(4):
                    bank = nb()
                    def mmf(e, ri=ri, ct=ct, bank=bank, fslot=fslot):
                        for lc in range(16):
                            arena = c.B1 if lc < 8 else c.B2
                            e.matmul(c.ps[bank][:], c.WB[:, fslot+2048*ri+128*lc:fslot+2048*ri+128*(lc+1)],
                                     arena[:, 2048*(lc % 8) + 512*ct:2048*(lc % 8) + 512*(ct+1)],
                                     start=(lc == 0), stop=(lc == 15))
                        return pe_fence(e, c)
                    mmop = P.op("tensor", mmf, reads=[f"ac_fst{par}", "ac_xcvT", "ident"],
                                writes=[f"ps{bank}", "psfence"], tag=f"ac_fmm{b}_{fp}_{ri}_{ct}")
                    if fpend is not None:
                        fwd_evac(fpend, mmop)
                        if fpend[1] == 1 and fpend[2] == 3:
                            pw_spill(fpend[0])
                    fpend = (fp, ri, ct, bank)
        tailf = P.op("tensor", lambda e: tail_fence(e, c), reads=["ident"],
                     writes=["psfence"], tag=f"ac_ftail{b}")
        fwd_evac(fpend, tailf)
        pw_spill(16)
        P.barrier()
        P.op("sync", lambda e: e.dma_start(
                c.XLNF[:].rearrange("p (q o) -> p q o", q=16),
                woutb.rearrange("(q p) o -> p q o", p=128)),
             reads=["ac_woutb"], writes=["xlnfm", "ac_wout"], dma=True, tag=f"ac_i{b}_wo")
        P.op("sync", lambda e: e.dma_start(c.B3[0:1, 12288:14336], W["Gc"][2048:2049, :]),
             writes=["ac_gcn"], dma=True, tag=f"ac_i{b}_gcn")
        P.op("sync", lambda e: e.dma_start(c.B3[0:1, 14336:16384], ysr[2048:2049, :]),
             reads=["ac_ysd"], writes=["ac_ynyq"], dma=True, tag=f"ac_i{b}_nyq")
        rpend = None
        for lq in range(4):
            arena = c.B1 if lq % 2 == 0 else c.B2
            gkey = f"ac_gq{lq%2}"
            for mi, mname in ((0, "Gc"), (1, "Gs")):
                P.op("sync", lambda e, mi=mi, mname=mname, lq=lq, arena=arena: e.dma_start(
                        arena[:, 8192*mi:8192*(mi+1)].rearrange("p (f r) -> p f r", f=16),
                        W[mname].rearrange("(f p) l -> p f l", p=128)[:, 0:16, 512*lq:512*(lq+1)]),
                     writes=[gkey], dma=True, tag=f"ac_ig{b}_{lq}_{mi}")
            for ct in range(4):
                def ldy(fp, ct=ct, b=b, lq=lq):
                    par = fp % 4
                    P.op("sync", lambda e, fp=fp, par=par, ct=ct: e.dma_start(
                            c.WB[:, 1024*par:1024*par+512], ysr[128*fp:128*(fp+1), 512*ct:512*(ct+1)]),
                         reads=["ac_ysd"], writes=[f"ac_yst{par}"], dma=True,
                         tag=f"ac_iyr{b}_{lq}_{ct}_{fp}")
                    P.op("sync", lambda e, fp=fp, par=par, ct=ct: e.dma_start(
                            c.WB[:, 1024*par+512:1024*(par+1)], ysi[128*fp:128*(fp+1), 512*ct:512*(ct+1)]),
                         reads=["ac_ysd"], writes=[f"ac_yst{par}"], dma=True,
                         tag=f"ac_iyi{b}_{lq}_{ct}_{fp}")
                for pf in range(3):
                    ldy(pf)
                for fp in range(16):
                    if fp < 13:
                        ldy(fp + 3)
                    par = fp % 4
                    def mmi(e, fp=fp, par=par, arena=arena):
                        last = None
                        for lb in range(4):
                            e.matmul(c.ps[lb][:], arena[:, 512*fp+128*lb:512*fp+128*(lb+1)],
                                     c.WB[:, 1024*par:1024*par+512],
                                     start=(fp == 0), stop=False)
                            last = e.matmul(c.ps[lb][:], arena[:, 8192+512*fp+128*lb:8192+512*fp+128*(lb+1)],
                                            c.WB[:, 1024*par+512:1024*(par+1)],
                                            start=False, stop=False)
                        return last
                    P.op("tensor", mmi, reads=[gkey, f"ac_yst{par}"],
                         writes=["ps0", "ps1", "ps2", "ps3"], tag=f"ac_imm{b}_{lq}_{ct}_{fp}")
                def mmn(e, lq=lq, ct=ct):
                    for lb in range(4):
                        e.matmul(c.ps[lb][:], c.B3[0:1, 12288+512*lq+128*lb:12288+512*lq+128*(lb+1)],
                                 c.B3[0:1, 14336+512*ct:14336+512*(ct+1)],
                                 start=False, stop=True)
                    return pe_fence(e, c)
                P.op("tensor", mmn, reads=["ac_gcn", "ac_ynyq", "ident"],
                     writes=["ps0", "ps1", "ps2", "ps3", "psfence"], tag=f"ac_inq{b}_{lq}_{ct}")
                tailop = P.op("tensor", lambda e: tail_fence(e, c), reads=["ident"],
                              writes=["psfence"], tag=f"ac_itl{b}_{lq}_{ct}")
                zpar = ct % 2
                for lb in range(4):
                    zcol = 4096*zpar + 512*lb
                    P.op("sync", lambda e, lq=lq, lb=lb, ct=ct, zcol=zcol: e.dma_start(
                            FWB[:, zcol:zcol+512],
                            zscr[128*(4*lq+lb):128*(4*lq+lb+1), 512*ct:512*(ct+1)]),
                         reads=["ac_zscrd"], writes=[f"ac_zt{zpar}_{lb}"], dma=True,
                         tag=f"ac_izl{b}_{lq}_{ct}_{lb}")
                    P.op("vector", lambda e, lb=lb, ct=ct, zcol=zcol: e.tensor_tensor(
                            c.B3[:, 2048*lb+512*ct:2048*lb+512*(ct+1)], c.ps[lb][:],
                            FWB[:, zcol:zcol+512], ALU.mult),
                         reads=[f"ps{lb}", f"ac_zt{zpar}_{lb}"], writes=["ac_yg"],
                         after=[tailop], tag=f"ac_ig8{b}_{lq}_{ct}_{lb}")
            for lb in range(4):
                i_tok = 4*lq + lb
                def tg(e, lb=lb):
                    for k in range(16):
                        dstb = c.pt[0] if k < 8 else c.pt[1]
                        e.transpose(dstb[:, 128*(k % 8):128*(k % 8 + 1)],
                                    c.B3[:, 2048*lb+128*k:2048*lb+128*(k+1)], c.ident)
                    return pe_fence(e, c)
                tgop = P.op("tensor", tg, reads=["ac_yg", "ident"],
                            writes=["pt0", "pt1", "psfence"], tag=f"ac_itg{b}_{lq}_{lb}")
                ypar = lb % 2
                for half in range(2):
                    P.op("vector", lambda e, ypar=ypar, half=half: e.tensor_copy(
                            c.B3[:, 8192+2048*ypar+1024*half:8192+2048*ypar+1024*(half+1)],
                            c.pt[half][:]),
                         reads=[f"pt{half}"], writes=[f"ac_ygt{ypar}"], after=[tgop],
                         tag=f"ac_igev{b}_{lq}_{lb}_{half}")
                banks = []
                for nt in range(2):
                    bank = nb()
                    def mmo(e, nt=nt, bank=bank, ypar=ypar):
                        for q in range(16):
                            e.matmul(c.ps[bank][:], c.B3[:, 8192+2048*ypar+128*q:8192+2048*ypar+128*(q+1)],
                                     c.XLNF[:, 1024*q+512*nt:1024*q+512*(nt+1)],
                                     start=(q == 0), stop=(q == 15))
                        return pe_fence(e, c)
                    mmop = P.op("tensor", mmo, reads=[f"ac_ygt{ypar}", "ac_wout", "ident"],
                                writes=[f"ps{bank}", "psfence"], tag=f"ac_omm{b}_{lq}_{lb}_{nt}")
                    if nt == 0 and rpend is not None:
                        pi, pb = rpend
                        emit_resid_store(P, c, xb, ob, pi, (c.ps[pb[0]][:], c.ps[pb[1]][:]),
                                         (f"ps{pb[0]}", f"ps{pb[1]}"), after=[mmop], tag=f"ac_r{b}_")
                        rpend = None
                    banks.append(bank)
                rpend = (i_tok, tuple(banks))
            tlq = P.op("tensor", lambda e: tail_fence(e, c), reads=["ident"],
                       writes=["psfence"], tag=f"ac_rtl{b}_{lq}")
            pi, pb = rpend
            emit_resid_store(P, c, xb, ob, pi, (c.ps[pb[0]][:], c.ps[pb[1]][:]),
                             (f"ps{pb[0]}", f"ps{pb[1]}"), after=[tlq], tag=f"ac_r{b}_")
            rpend = None

def make_scratch(nc):
    return {
        "w1t": nc.dram_tensor("scr_w1t", [1024, 4096], BF16, kind="Internal").ap(),
        "w2t": nc.dram_tensor("scr_w2t", [4096, 1024], BF16, kind="Internal").ap(),
        "hbuf": nc.dram_tensor("scr_hbuf", [4*4096, 2048], BF16, kind="Internal").ap(),
        "bounce": nc.dram_tensor("scr_bounce", [1, 4], F32, kind="Internal").ap(),
        "bounce128": nc.dram_tensor("scr_bounce128", [128, 1], F32, kind="Internal").ap(),
        "bounce_row": nc.dram_tensor("scr_brow", [2, 1032], F32, kind="Internal").ap(),
        "bounce_pt": nc.dram_tensor("scr_bpt", [128, 64], F32, kind="Internal").ap(),
        "winb": nc.dram_tensor("scr_winb", [1024, 4096], BF16, kind="Internal").ap(),
        "woutb": nc.dram_tensor("scr_woutb", [2048, 1024], BF16, kind="Internal").ap(),
        "zscr": nc.dram_tensor("scr_zscr", [2048, 2048], BF16, kind="Internal").ap(),
        "ysr": nc.dram_tensor("scr_ysr", [2176, 2048], BF16, kind="Internal").ap(),
        "ysi": nc.dram_tensor("scr_ysi", [2176, 2048], BF16, kind="Internal").ap(),
        "kfr": nc.dram_tensor("scr_kfr", [2176, 2048], BF16, kind="Internal").ap(),
        "kfi": nc.dram_tensor("scr_kfi", [2176, 2048], BF16, kind="Internal").ap(),
    }

build_single_debug = False
build_single_extra_dumps = None

def build_single(sub_name, weight_shapes):
    global DEBUG_STASH
    nc = bass.Bass("TRN2", target_bir_lowering=False)
    x = nc.dram_tensor("x", [B*S, D], F32, kind="ExternalInput").ap()
    out = nc.dram_tensor("out", [B*S, D], F32, kind="ExternalOutput").ap()
    dbg_tail = None
    if build_single_debug:
        DEBUG_STASH = {}
        dbg_tail = nc.dram_tensor("dbg_tail", [16, 320], BF16, kind="ExternalOutput").ap()
    consts_np = host_consts()
    W = {}
    for name, shape, dt in weight_shapes:
        W[name] = nc.dram_tensor(name, list(shape), dt, kind="ExternalInput").ap()
    for name, arr in consts_np.items():
        dt = F32 if arr.dtype == np.float32 else BF16
        W["const_" + name] = nc.dram_tensor("const_" + name, list(arr.shape), dt,
                                            kind="ExternalInput").ap()
    with ExitStack() as stack:
        E = stack.enter_context
        c = alloc_common(nc, E)
        sems = make_sems(nc, E)
        block = E(nc.Block())
        P = Plan()
        load_consts(P, c, {"ident": W["const_ident"], "i2": W["const_i2"],
                           "sel": W["const_sel"], "identf": W["const_identf"]})
        P.barrier()
        if sub_name in ("L0_ffn", "L1_ffn", "L1_attn_a"):
            EMITTERS[sub_name](P, c, x, out, W, make_scratch(nc))
        else:
            EMITTERS[sub_name](P, c, x, out, W)
        if build_single_debug and dbg_tail is not None:
            P.barrier()
            P.op("sync", lambda e: e.dma_start(dbg_tail, c.B3[0:16, 16384:16704]),
                 dma=True, tag="dbgtail")
        if build_single_extra_dumps:
            P.barrier()
            for dname, attr in build_single_extra_dumps:
                buf = getattr(c, attr)
                dshape = [buf.tensor.shape[0] if hasattr(buf, 'tensor') else 128, buf.shape[1]]
                dt_ = buf.dtype
                dout = nc.dram_tensor(dname, [128, buf.shape[1]], dt_, kind="ExternalOutput").ap()
                P.op("sync", lambda e, dout=dout, buf=buf: e.dma_start(dout, buf[:]),
                     dma=True, tag=f"dump_{dname}")
        P.barrier()
        P.op("sync", lambda e: None, tag="fin")
        P.emit(nc, block, sems, echo_fns=make_echo_fns(c))
    return nc, consts_np

EMITTERS = {
    "L1_attn_b": emit_L1_attn_b,
    "L0_ffn": emit_L0_ffn,
    "L1_ffn": emit_L1_ffn,
    "L0_attn_a": emit_L0_attn_a,
    "L0_attn_b": emit_L0_attn_b,
    "L1_attn_a": emit_L1_attn_a,
}

def _host_relayout(inputs):
    g = lambda k: np.ascontiguousarray(inputs[k])
    W = {}
    W["L0_attn_a"] = {
        "Wq": g("L0_attn_a_Wq"), "Wk": g("L0_attn_a_Wk"), "Wv": g("L0_attn_a_Wv"),
        "Wg": g("L0_attn_a_Wg"), "bg": g("L0_attn_a_bg"), "Wo": g("L0_attn_a_Wo"),
    }
    W["L0_ffn"] = {
        "W1T": np.ascontiguousarray(np.asarray(inputs["L0_ffn_W1"]).T),
        "W2T": np.ascontiguousarray(np.asarray(inputs["L0_ffn_W2"]).T),
        "b1r": np.ascontiguousarray(np.asarray(inputs["L0_ffn_b1"]).reshape(32, 128).T),
        "b2": g("L0_ffn_b2"),
    }
    W["L0_attn_b"] = {
        "Wq": g("L0_attn_b_Wq"), "Wk": g("L0_attn_b_Wk"),
        "Wv2": np.ascontiguousarray(np.asarray(inputs["L0_attn_b_Wv"]).transpose(1, 0, 2).reshape(1024, 1024)),
    }
    cw = np.asarray(inputs["L1_attn_a_conv_w"])
    cb = np.asarray(inputs["L1_attn_a_conv_b"])
    W["L1_attn_a"] = {
        "winT": np.ascontiguousarray(np.asarray(inputs["L1_attn_a_W_in"]).T),
        "woutT": np.ascontiguousarray(np.asarray(inputs["L1_attn_a_W_out"]).T),
        "kT": np.ascontiguousarray(np.asarray(inputs["L1_attn_a_k_ssm"]).T),
        "cwr": np.ascontiguousarray(cw.reshape(16, 128, 4).transpose(1, 0, 2).reshape(128, 64)),
        "cbr": np.ascontiguousarray(cb.reshape(16, 128).T),
        "dsk": np.ascontiguousarray(np.asarray(inputs["L1_attn_a_D_skip"]).reshape(1, 2048)),
    }
    W["L1_attn_a"].update(host_dft_consts())
    bd1 = np.asarray(inputs["L1_ffn_bd1"])
    bd2 = np.asarray(inputs["L1_ffn_bd2"])
    W["L1_ffn"] = {
        "W_base": g("L1_ffn_W_base"), "b_base": g("L1_ffn_b_base"),
        "Wd1": g("L1_ffn_Wd1"),
        "bd1r": np.ascontiguousarray(bd1.reshape(8, 2, 128).transpose(2, 0, 1).reshape(128, 16)),
        "Wd2n": g("L1_ffn_Wd2"),
        "Wd2T": np.ascontiguousarray(np.asarray(inputs["L1_ffn_Wd2"]).transpose(0, 2, 1)),
        "bd2f2": np.ascontiguousarray(bd2.reshape(2, 4096)),
        "bd2c": np.ascontiguousarray(bd2.reshape(8, 8, 128).transpose(2, 0, 1).reshape(128, 64)),
        "Wa_down": g("L1_ffn_Wa_down"),
        "ba_downf": np.ascontiguousarray(np.asarray(inputs["L1_ffn_ba_down"]).reshape(1, 512)),
        "Wa_up": g("L1_ffn_Wa_up"),
        "ba_upf": np.ascontiguousarray(np.asarray(inputs["L1_ffn_ba_up"]).reshape(1, 8192)),
    }
    W["L1_attn_b"] = {
        "Wq": g("L1_attn_b_Wq"), "Wk": g("L1_attn_b_Wk"), "Wv": g("L1_attn_b_Wv"),
        "Wo": g("L1_attn_b_Wo"), "bo": g("L1_attn_b_bo"),
    }
    return W

_SUB_SHAPES = {
    "L0_attn_a": [("Wq", (1024, 1024), F32), ("Wk", (1024, 1024), F32),
                  ("Wv", (1024, 1024), F32), ("Wg", (1024, 1024), F32),
                  ("bg", (1024,), F32), ("Wo", (1024, 1024), F32)],
    "L0_ffn": [("W1T", (1024, 4096), F32), ("W2T", (4096, 1024), F32),
               ("b1r", (128, 32), F32), ("b2", (1024,), F32)],
    "L0_attn_b": [("Wq", (16, 1024, 64), F32), ("Wk", (16, 1024, 64), F32),
                  ("Wv2", (1024, 1024), F32)],
    "L1_attn_a": [("winT", (1024, 4096), F32), ("woutT", (2048, 1024), F32),
                  ("kT", (4096, 2048), F32), ("cwr", (128, 64), F32),
                  ("cbr", (128, 16), F32), ("dsk", (1, 2048), F32),
                  ("Fc", (2048, 2176), BF16), ("Fs", (2048, 2176), BF16),
                  ("Fc2", (2048, 2176), BF16), ("Fs2", (2048, 2176), BF16),
                  ("Gc", (2176, 2048), BF16), ("Gs", (2176, 2048), BF16)],
    "L1_ffn": [("W_base", (1024, 1024), F32), ("b_base", (1024,), F32),
               ("Wd1", (8, 1024, 256), F32), ("bd1r", (128, 16), F32),
               ("Wd2n", (8, 256, 1024), F32), ("Wd2T", (8, 1024, 256), F32),
               ("bd2f2", (2, 4096), F32), ("bd2c", (128, 64), F32),
               ("Wa_down", (8, 1024, 64), F32), ("ba_downf", (1, 512), F32),
               ("Wa_up", (8, 64, 1024), F32), ("ba_upf", (1, 8192), F32)],
    "L1_attn_b": [("Wq", (1024, 1024), F32), ("Wk", (1024, 1024), F32),
                  ("Wv", (1024, 1024), F32), ("Wo", (1024, 1024), F32),
                  ("bo", (1024,), F32)],
}

CHAIN = ("L0_attn_a", "L0_ffn", "L0_attn_b", "L0_ffn",
         "L1_attn_a", "L1_ffn", "L1_attn_b", "L1_ffn")

def build_fused():
    nc = bass.Bass("TRN2", target_bir_lowering=False)
    x = nc.dram_tensor("x", [B*S, D], F32, kind="ExternalInput").ap()
    out = nc.dram_tensor("out", [B*S, D], F32, kind="ExternalOutput").ap()
    r1 = nc.dram_tensor("resid1", [B*S, D], F32, kind="Internal").ap()
    r2 = nc.dram_tensor("resid2", [B*S, D], F32, kind="Internal").ap()
    consts_np = host_consts()
    Wall = {}
    for sub in dict.fromkeys(CHAIN):
        Wall[sub] = {name: nc.dram_tensor(sub + "__" + name, list(shape), dt,
                                          kind="ExternalInput").ap()
                     for name, shape, dt in _SUB_SHAPES[sub]}
    Wc = {}
    for name, arr in consts_np.items():
        dt = F32 if arr.dtype == np.float32 else BF16
        Wc[name] = nc.dram_tensor("const_" + name, list(arr.shape), dt,
                                  kind="ExternalInput").ap()
    io = [x, r1, r2, r1, r2, r1, r2, r1, out]
    with ExitStack() as stack:
        E = stack.enter_context
        c = alloc_common(nc, E)
        sems = make_sems(nc, E)
        block = E(nc.Block())
        P = Plan()
        load_consts(P, c, {"ident": Wc["ident"], "i2": Wc["i2"],
                           "sel": Wc["sel"], "identf": Wc["identf"]})
        P.barrier()
        scratch = make_scratch(nc)
        PHASE_MARKS.clear()
        for k, sub in enumerate(CHAIN):
            P.barrier()
            PHASE_MARKS.append((f"{k}:{sub}", {q: len(P.ops[q]) for q in QUEUES}))
            src, dst = io[k], io[k+1]
            if sub == "L0_ffn":
                emit_L0_ffn(P, c, src, dst, Wall[sub], scratch, prep=(k == 1))
            elif sub == "L1_ffn":
                emit_L1_ffn(P, c, src, dst, Wall[sub], scratch)
            elif sub == "L1_attn_a":
                emit_L1_attn_a(P, c, src, dst, Wall[sub], scratch, prep=True)
            else:
                EMITTERS[sub](P, c, src, dst, Wall[sub])
        P.barrier()
        P.op("sync", lambda e: None, tag="fin")
        P.emit(nc, block, sems, echo_fns=make_echo_fns(c))
    LAST_PLAN.clear()
    LAST_PLAN.append(P)
    return nc, consts_np

FUSED = True
PHASE_MARKS = []
LAST_PLAN = []

def kernel(**inputs):
    NCORES = 8
    Wmaps = _host_relayout(inputs)
    x0 = np.asarray(inputs["L0_attn_a_x"], dtype=np.float32)
    cur = [np.ascontiguousarray(x0[4*i:4*(i+1)].reshape(B*S, D)) for i in range(NCORES)]
    if FUSED:
        nc, consts_np = build_fused()
        base = {}
        for sub in dict.fromkeys(CHAIN):
            for name, arr in Wmaps[sub].items():
                base[sub + "__" + name] = arr
        for n, a in consts_np.items():
            base["const_" + n] = a
        in_maps = []
        for i in range(NCORES):
            m = dict(base)
            m["x"] = cur[i]
            in_maps.append(m)
        res = run_bass_kernel_spmd(nc, in_maps, core_ids=list(range(NCORES)))
        cur = [np.asarray(res.results[i]["out"], dtype=np.float32) for i in range(NCORES)]
    else:
        progs = {}
        for sub in dict.fromkeys(CHAIN):
            progs[sub] = build_single(sub, _SUB_SHAPES[sub])
        for sub in CHAIN:
            nc, consts_np = progs[sub]
            base = dict(Wmaps[sub])
            for n, a in consts_np.items():
                base["const_" + n] = a
            in_maps = []
            for i in range(NCORES):
                m = dict(base)
                m["x"] = cur[i]
                in_maps.append(m)
            res = run_bass_kernel_spmd(nc, in_maps, core_ids=list(range(NCORES)))
            cur = [np.asarray(res.results[i]["out"], dtype=np.float32) for i in range(NCORES)]
    out = np.concatenate([c.reshape(4, S, D) for c in cur], axis=0)
    return out
